# version 08 (final)
# speedup vs baseline: 1.7800x; 1.0487x over previous
import sys
import numpy as np
import ml_dtypes

try:
    import concourse.bass as bass
    import concourse.mybir as mybir
    from concourse.bass_utils import run_bass_kernel_spmd
    from concourse.bass_core_rust import IndirectOffsetOnAxis as IOX
except ImportError:
    sys.path.insert(0, "/opt/trn_rl_repo")
    import concourse.bass as bass
    import concourse.mybir as mybir
    from concourse.bass_utils import run_bass_kernel_spmd
    from concourse.bass_core_rust import IndirectOffsetOnAxis as IOX

from contextlib import ExitStack

BF16 = mybir.dt.bfloat16
F32 = mybir.dt.float32
AF = mybir.ActivationFunctionType
ALU = mybir.AluOpType
AX = mybir.AxisListType
bfdt = ml_dtypes.bfloat16

D, DT, TOK, BL, NB, TKV, NKB = 2048, 16, 2048, 512, 4, 2560, 5
NH, HG, NGRP, DH, RS = 16, 4, 4, 128, 16
FI, NIT, NE, SI, NEXP = 1408, 11, 16, 2816, 16
NIT2 = 22
CAP, NSLOT, ROWW = 384, 16 * 384, 2064
NT = CAP // 128
HUGE_OFF = 1.0e6
INV_SQ_DH = 1.0 / float(np.sqrt(128.0))
LORA_SC = 1.0 / 16.0
EPS_LN, EPS_RMS, NEG_BIG = 1e-5, 1e-6, -1.0e30

PHASES = 6


def build_program(phases=PHASES, debug=False, fstop=3, SCAT_ON=False, CHUNK_ON=False, CHL=9, DBGI=False):
    nc = bass.Bass("TRN2", target_bir_lowering=False, debug=False,
                   detect_race_conditions=False)

    def dram_in(name, shape, dt):
        return nc.dram_tensor(name, list(shape), dt, kind="ExternalInput").ap()

    xe = dram_in("xe", (DT, 128, TKV), F32)
    masks_d = dram_in("masks", (5, 128, 640), BF16)
    cos_d = dram_in("costab", (128, TKV), BF16)
    sin_d = dram_in("sintab", (128, TKV), BF16)
    wqT_d = dram_in("wqT", (D, D), F32)
    wkT_d = dram_in("wkT", (D, D), F32)
    wvT_d = dram_in("wvT", (D, D), F32)
    woT_d = dram_in("woT", (D, D), F32)
    qkvAT_d = dram_in("qkvAT", (D, 96), F32)
    qBT_d = dram_in("qBT", (RS, D), F32)
    kBT_d = dram_in("kBT", (RS, D), F32)
    vBT_d = dram_in("vBT", (RS, D), F32)
    oAT_d = dram_in("oAT", (D, RS), F32)
    oBT_d = dram_in("oBT", (RS, D), F32)
    nwT_d = dram_in("nwT", (128, DT), F32)
    gwT_d = dram_in("gwT", (D, NE), F32)
    w1T_d = dram_in("w1T", (NE, D, FI), BF16)
    w3T_d = dram_in("w3T", (NE, D, FI), BF16)
    w2T_d = dram_in("w2T", (NE, FI, D), BF16)
    sw1T_d = dram_in("sw1T", (D, SI), BF16)
    sw3T_d = dram_in("sw3T", (D, SI), BF16)
    sw2T_d = dram_in("sw2T", (SI, D), BF16)
    idb_d = dram_in("idb", (128, 128), BF16)
    idf_d = dram_in("idf", (128, 128), F32)
    id16f_d = dram_in("id16f", (16, 16), F32)
    stu_d = dram_in("stu", (128, 128), F32)
    ecap_d = dram_in("ecap", (128, 16), F32)
    eidx_d = dram_in("eidx", (128, 16), F32)
    si4_d = dram_in("si4", (128, 4), F32)
    io1_d = dram_in("io1", (128, 1), F32)
    onerb_d = dram_in("onerb", (1, 128), BF16)
    onerf_d = dram_in("onerf", (1, 128), F32)
    onecb_d = dram_in("onecb", (128, 1), BF16)
    onecf_d = dram_in("onecf", (128, 1), F32)
    Serow_d = dram_in("serow_in", (1, 16), F32)

    sk = "ExternalOutput" if debug else "Internal"
    xn_scr = nc.dram_tensor("xn_scr", [NKB, DT, 128, BL], BF16, kind=sk).ap()
    attn_scr = nc.dram_tensor("attn_scr", [NB, NH, 128, BL], BF16, kind=sk).ap()
    x1_scr = nc.dram_tensor("x1_scr", [NB, DT, 128, BL], F32, kind=sk).ap()
    row_scr = nc.dram_tensor("row_scr", [2, NKB, BL], F32, kind=sk).ap()
    Gall_d = nc.dram_tensor("Gall", [NSLOT + 32, ROWW], BF16, kind=sk).ap()
    arena_d = nc.dram_tensor("arena", [2 * TOK + 64, D], F32, kind=sk).ap()
    xn2_scr = nc.dram_tensor("xn2_scr", [2, DT, 128, 1024], BF16, kind=sk).ap()
    yTok_d = nc.dram_tensor("yTok", [TOK, D], F32, kind="ExternalOutput").ap()
    dbgi_d = nc.dram_tensor("dbgi", [16, 2, 128, 1], mybir.dt.int32,
                            kind="ExternalOutput" if DBGI else "Internal").ap()
    dbgf_d = nc.dram_tensor("dbgf", [16, 3, 128, 16], F32,
                            kind="ExternalOutput" if DBGI else "Internal").ap()
    dbgz_d = nc.dram_tensor("dbgz", [4, 128, 64], F32,
                            kind="ExternalOutput" if DBGI else "Internal").ap()
    dbgv_d = nc.dram_tensor("dbgv", [4, 128, 16], F32,
                            kind="ExternalOutput" if DBGI else "Internal").ap()
    dbgm_d = nc.dram_tensor("dbgm", [4, 2, 128, 64], F32,
                            kind="ExternalOutput" if DBGI else "Internal").ap()

    st = ExitStack()
    ec = st.enter_context

    class Sem:
        def __init__(self, name):
            self.h = ec(nc.semaphore(name))
            self.n = 0

    DMA_SEMS = []

    def dsem(name):
        sm = Sem(name)
        DMA_SEMS.append(sm)
        return sm

    P0L = dsem("P0L"); SSE = dsem("SSE"); QAL = dsem("QAL")
    XE = [dsem("XE0"), dsem("XE1")]
    RWS = dsem("RWS")
    XNS = [dsem(f"XNS{i}") for i in range(NKB)]
    ML = dsem("ML"); LBL = dsem("LBL")
    COS = [dsem("COS0"), dsem("COS1")]
    WS = [dsem("WS0"), dsem("WS1")]
    XNB = [dsem("XNB0"), dsem("XNB1")]
    ATH = [dsem("ATH0"), dsem("ATH1")]
    OAL = dsem("OAL"); ATL = dsem("ATL")
    XC = [dsem("XC0"), dsem("XC1")]
    RWL = dsem("RWL")
    X1S = [dsem("X1S0"), dsem("X1S1")]
    CSTL = dsem("CSTL"); GZ = dsem("GZ"); GY = dsem("GY")
    GSC = [dsem("GSC0"), dsem("GSC1")]
    XN2S = [dsem("XN2S0"), dsem("XN2S1")]
    GL = [dsem("GL0"), dsem("GL1")]
    W13 = [dsem(f"W13_{x}") for x in range(4)]
    W2L = [dsem("W2L0"), dsem("W2L1")]
    GYS = [dsem("GYS0"), dsem("GYS1")]
    XSL = [dsem("XSL0"), dsem("XSL1")]
    WH = [dsem("WH0"), dsem("WH1")]
    W2H = [dsem("W2H0"), dsem("W2H1")]
    A1L = dsem("A1L"); A2L = dsem("A2L")
    X1T = [dsem("X1T0"), dsem("X1T1")]
    OUTS = [dsem("OUTS0"), dsem("OUTS1")]
    DBG_SEM = dsem("DBGS")
    PE = Sem("sPE")
    VE = Sem("sVE")
    AC = Sem("sAC")
    GP = Sem("sGP")
    RW = Sem("sRW")
    PH = Sem("sPH")

    block = ec(nc.Block())
    ENGS = ["sync", "tensor", "vector", "scalar", "gpsimd"]

    def step(eng, waits, build, inc=None, k=1):
        if inc is not None:
            inc.n += k
        tgt = inc.n if inc is not None else None
        w = [(s, v) for (s, v) in waits if v]

        def body(q, w=w, build=build, inc=inc, k=k):
            for s, v in w:
                q.wait_ge(s.h, v)
            last = build(q)
            if inc is not None:
                last.then_inc(inc.h, k)
        getattr(block, eng)(body)
        return tgt

    def vseq(q, ops):
        last = None
        for i, op in enumerate(ops):
            if i > 0:
                RW.n += 1
                last.then_inc(RW.h, 1)
                q.wait_ge(RW.h, RW.n)
            last = op(q)
        return last

    def barrier():
        base = PH.n
        for e in ENGS:
            step(e, [], lambda q: q.nop(), inc=PH)
        dws = [(sm, sm.n) for sm in DMA_SEMS]
        for e in ENGS:
            step(e, [(PH, base + len(ENGS))] + dws, lambda q: q.nop())

    idb = ec(nc.sbuf_tensor("idb_s", [128, 128], BF16))
    idf = ec(nc.sbuf_tensor("idf_s", [128, 128], F32))
    id16f = ec(nc.sbuf_tensor("id16f_s", [16, 16], F32))
    onerb = ec(nc.sbuf_tensor("onerb_s", [1, 128], BF16))
    onerf = ec(nc.sbuf_tensor("onerf_s", [1, 128], F32))
    onecb = ec(nc.sbuf_tensor("onecb_s", [128, 1], BF16))
    onecf = ec(nc.sbuf_tensor("onecf_s", [128, 1], F32))
    nwT = ec(nc.sbuf_tensor("nwT_s", [128, DT], F32))
    nwTq = ec(nc.sbuf_tensor("nwTq_s", [128, DT], F32))
    gw_s = ec(nc.sbuf_tensor("gw_s", [128, DT * 16], F32))
    Se_row = ec(nc.sbuf_tensor("Se_row", [1, 16], F32))
    Se_col = ec(nc.sbuf_tensor("Se_col", [16, 1], F32))
    rows = ec(nc.sbuf_tensor("rows", [1, 8 * BL], F32))
    RM, RQ, RT, RV, RR, RC, RE, RT2 = [i * BL for i in range(8)]
    ws_scope = ExitStack()
    wstage0 = ws_scope.enter_context(nc.sbuf_tensor("wstage0", [128, DT * 128], F32))
    wstage1 = ws_scope.enter_context(nc.sbuf_tensor("wstage1", [128, DT * 128], F32))
    wst = [wstage0, wstage1]

    with nc.psum_tensor("psSe", [16, 512], F32) as psSe:
        def p0_loads(q):
            for sb, dr in [(idb, idb_d), (idf, idf_d), (id16f, id16f_d),
                           (onerb, onerb_d), (onerf, onerf_d), (onecb, onecb_d),
                           (onecf, onecf_d), (nwT, nwT_d), (Se_row, Serow_d)]:
                q.dma_start(out=sb[:], in_=dr[:]).then_inc(P0L.h, 16)
            return q.dma_start(out=gw_s[:].rearrange("p (a e) -> p a e", e=16),
                               in_=gwT_d.rearrange("(a p) e -> p a e", p=128))
        P0L.n += 16 * 9
        p0_ld = step("sync", [], p0_loads, inc=P0L, k=16)

        ve_nwq = step("vector", [(P0L, p0_ld)],
                      lambda q: q.tensor_scalar_mul(nwTq[:], nwT[:], INV_SQ_DH),
                      inc=VE)

        def p0_se(q):
            last = None
            for a in range(DT):
                last = q.matmul(psSe[0:16, 0:1], gw_s[:, a * 16:(a + 1) * 16],
                                onecf[:], start=(a == 0), stop=(a == DT - 1))
            return last
        p0_pe = step("tensor", [(P0L, p0_ld)], p0_se, inc=PE)
        p0_sev = step("vector", [(PE, p0_pe)],
                      lambda q: q.tensor_copy(Se_col[:], psSe[0:16, 0:1]), inc=VE)

        def p0_serow(q):
            last = None
            for e in range(16):
                last = q.dma_start(out=Se_row[0:1, e:e + 1], in_=Se_col[e:e + 1, 0:1])
                if e < 15:
                    last.then_inc(SSE.h, 16)
            return last
        SSE.n += 16 * 15
        p0_se_done = step("scalar", [(VE, p0_sev)], p0_serow, inc=SSE, k=16)

    sc_xn = [0] * NKB
    sc_row = 0
    sc_attn_all = 0
    pa_scope = ExitStack()
    pec = pa_scope.enter_context
    xnb = [pec(nc.sbuf_tensor(f"xnb{i}", [128, DT * BL], BF16)) for i in range(2)]
    u_sb = pec(nc.sbuf_tensor("u_sb", [16, 3 * TKV], BF16))

    if phases >= 1:
        p1_scope = ExitStack()
        _aec = p1_scope.enter_context
        xblk0 = _aec(nc.sbuf_tensor("xblk0", [128, DT * BL], F32))
        xblk1 = _aec(nc.sbuf_tensor("xblk1", [128, DT * BL], F32))
        qkvA_b = _aec(nc.sbuf_tensor("qkvA_b", [128, DT * 96], BF16))
        xc0 = _aec(nc.sbuf_tensor("xcst0", [128, BL], BF16))
        xc1 = _aec(nc.sbuf_tensor("xcst1", [128, BL], BF16))
        xs0 = _aec(nc.sbuf_tensor("xsq0", [128, BL], BF16))
        xs1 = _aec(nc.sbuf_tensor("xsq1", [128, BL], BF16))
        tmp1 = _aec(nc.sbuf_tensor("tmp1", [128, BL], F32))
        psSx = _aec(nc.psum_tensor("psSx", [128, 512], F32))
        psSq = _aec(nc.psum_tensor("psSq", [128, 512], F32))
        psBcC = _aec(nc.psum_tensor("psBcC", [128, 512], F32))
        psBcE = _aec(nc.psum_tensor("psBcE", [128, 512], F32))
        psU = _aec(nc.psum_tensor("psU", [96, 512], F32))
        if True:

            ld_qa = step("sync", [(SSE, p0_se_done)],
                         lambda q: q.dma_start(
                             out=wstage0[:, 0:DT * 96].rearrange("p (a o) -> p a o", o=96),
                             in_=qkvAT_d.rearrange("(a p) o -> p a o", p=128)),
                         inc=QAL, k=16)

            def qa_fold(q):
                last = None
                for a in range(DT):
                    last = q.tensor_scalar_mul(qkvA_b[:, a * 96:(a + 1) * 96],
                                               wstage0[:, a * 96:(a + 1) * 96],
                                               nwT[:, a:a + 1])
                return last
            ve_qa = step("vector", [(QAL, ld_qa)], qa_fold, inc=VE)

            xblks = [xblk0, xblk1]
            xcs = [xc0, xc1]
            xss = [xs0, xs1]
            ve_xn = [0] * NKB
            ve_u = [0] * NKB
            row_done = [0] * NKB
            sc_row_prev = 0

            for jj in range(NKB):
                xb = xblks[jj % 2]
                xn = xnb[jj % 2]

                ld_x = step("sync", [(VE, ve_xn[jj - 2] if jj >= 2 else 0)],
                            lambda q, jj=jj, xb=xb: q.dma_start(
                                out=xb[:].rearrange("p (a t) -> p a t", t=BL),
                                in_=xe[:, :, jj * BL:(jj + 1) * BL].rearrange("a p t -> p a t")),
                            inc=XE[jj % 2], k=16)

                pe_tile = {}
                for a in range(DT):
                    sl = a % 2
                    w_ = [(XE[jj % 2], ld_x)]
                    if a >= 2:
                        w_.append((PE, pe_tile[a - 2]))

                    def cast_b(q, xb=xb, a=a, sl=sl):
                        q.activation(xcs[sl][:], xb[:, a * BL:(a + 1) * BL], AF.Copy)
                        return q.activation(xss[sl][:], xb[:, a * BL:(a + 1) * BL], AF.Square)
                    ac_v = step("scalar", w_, cast_b, inc=AC)

                    def stat_b(q, a=a, sl=sl):
                        q.matmul(psSx[0:1, :], onecb[:], xcs[sl][:],
                                 start=(a == 0), stop=(a == DT - 1))
                        return q.matmul(psSq[0:1, :], onecb[:], xss[sl][:],
                                        start=(a == 0), stop=(a == DT - 1))
                    w2_ = [(AC, ac_v)]
                    if a == 0 and jj > 0:
                        w2_.append((RW, row_done[jj - 1]))
                    pe_tile[a] = step("tensor", w2_, stat_b, inc=PE)
                pe_stat = pe_tile[DT - 1]

                def rstep(eng, build, extra=None):
                    w_ = [(RW, RW.n)]
                    if extra:
                        w_.append(extra)
                    return step(eng, w_, build, inc=RW)

                w0_ = [(RW, RW.n), (PE, pe_stat), (RWS, sc_row_prev)]
                step("vector", w0_, lambda q: q.tensor_scalar_mul(
                    rows[0:1, RM:RM + BL], psSx[0:1, :], 1.0 / D), inc=RW)
                rstep("vector", lambda q: q.tensor_scalar_mul(
                    rows[0:1, RQ:RQ + BL], psSq[0:1, :], 1.0 / D))
                rstep("vector", lambda q: q.tensor_mul(
                    rows[0:1, RT:RT + BL], rows[0:1, RM:RM + BL], rows[0:1, RM:RM + BL]))
                rstep("vector", lambda q: q.tensor_sub(
                    rows[0:1, RV:RV + BL], rows[0:1, RQ:RQ + BL], rows[0:1, RT:RT + BL]))
                rstep("vector", lambda q: q.tensor_scalar_add(
                    rows[0:1, RT:RT + BL], rows[0:1, RV:RV + BL], EPS_LN))
                rstep("scalar", lambda q: q.activation(
                    rows[0:1, RT:RT + BL], rows[0:1, RT:RT + BL], AF.Sqrt))
                rstep("vector", lambda q: q.reciprocal(
                    rows[0:1, RR:RR + BL], rows[0:1, RT:RT + BL]))
                rstep("vector", lambda q: q.tensor_mul(
                    rows[0:1, RT:RT + BL], rows[0:1, RV:RV + BL], rows[0:1, RR:RR + BL]))
                rstep("vector", lambda q: q.tensor_mul(
                    rows[0:1, RT:RT + BL], rows[0:1, RT:RT + BL], rows[0:1, RR:RR + BL]))
                rstep("vector", lambda q: q.tensor_scalar_add(
                    rows[0:1, RT:RT + BL], rows[0:1, RT:RT + BL], EPS_RMS))
                rstep("scalar", lambda q: q.activation(
                    rows[0:1, RT:RT + BL], rows[0:1, RT:RT + BL], AF.Sqrt))
                rstep("vector", lambda q: q.reciprocal(
                    rows[0:1, RT2:RT2 + BL], rows[0:1, RT:RT + BL]))
                rstep("vector", lambda q: q.tensor_mul(
                    rows[0:1, RC:RC + BL], rows[0:1, RR:RR + BL], rows[0:1, RT2:RT2 + BL]))
                rstep("vector", lambda q: q.tensor_mul(
                    rows[0:1, RE:RE + BL], rows[0:1, RM:RM + BL], rows[0:1, RC:RC + BL]))
                rstep("vector", lambda q: q.tensor_scalar_mul(
                    rows[0:1, RE:RE + BL], rows[0:1, RE:RE + BL], -1.0))
                rstep("vector", lambda q: q.tensor_scalar_add(
                    rows[0:1, RT:RT + BL], rows[0:1, RR:RR + BL], 1.0))
                rstep("vector", lambda q: q.tensor_mul(
                    rows[0:1, RT2:RT2 + BL], rows[0:1, RM:RM + BL], rows[0:1, RR:RR + BL]))
                row_done[jj] = rstep("vector", lambda q: q.tensor_scalar_mul(
                    rows[0:1, RT2:RT2 + BL], rows[0:1, RT2:RT2 + BL], -1.0))

                def rowdma(q, jj=jj):
                    q.dma_start(out=row_scr[0, jj].unsqueeze(0),
                                in_=rows[0:1, RT:RT + BL]).then_inc(RWS.h, 16)
                    return q.dma_start(out=row_scr[1, jj].unsqueeze(0),
                                       in_=rows[0:1, RT2:RT2 + BL])
                RWS.n += 16
                sc_row = step("scalar", [(RW, row_done[jj])], rowdma, inc=RWS, k=16)
                sc_row_prev = sc_row

                def bc_b(q):
                    q.matmul(psBcC[:, :], onerf[:], rows[0:1, RC:RC + BL],
                             start=True, stop=True)
                    return q.matmul(psBcE[:, :], onerf[:], rows[0:1, RE:RE + BL],
                                    start=True, stop=True)
                pe_bc = step("tensor",
                             [(RW, row_done[jj]), (VE, ve_xn[jj - 1] if jj >= 1 else 0)],
                             bc_b, inc=PE)

                def xn_b(q, xb=xb, xn=xn):
                    last = None
                    for a in range(DT):
                        q.tensor_mul(tmp1[:], xb[:, a * BL:(a + 1) * BL], psBcC[:, :])
                        last = q.tensor_add(xn[:, a * BL:(a + 1) * BL], tmp1[:], psBcE[:, :])
                    return last
                ve_xn[jj] = step("vector",
                                 [(PE, pe_bc), (XE[jj % 2], ld_x),
                                  (XNS[jj - 2] if jj >= 2 else XNS[0],
                                   sc_xn[jj - 2] if jj >= 2 else 0)],
                                 xn_b, inc=VE)

                def u_b(q, xn=xn):
                    last = None
                    for a in range(DT):
                        last = q.matmul(psU[:, :], qkvA_b[:, a * 96:(a + 1) * 96],
                                        xn[:, a * BL:(a + 1) * BL],
                                        start=(a == 0), stop=(a == DT - 1))
                    return last
                pe_u = step("tensor",
                            [(VE, ve_xn[jj]), (VE, ve_qa),
                             (VE, ve_u[jj - 1] if jj >= 1 else 0)],
                            u_b, inc=PE)

                def u_e(q, jj=jj):
                    q.tensor_scalar_mul(u_sb[0:16, jj * BL:(jj + 1) * BL],
                                        psU[0:16, :], LORA_SC)
                    q.tensor_scalar_mul(u_sb[0:16, TKV + jj * BL:TKV + (jj + 1) * BL],
                                        psU[32:48, :], LORA_SC)
                    return q.tensor_scalar_mul(
                        u_sb[0:16, 2 * TKV + jj * BL:2 * TKV + (jj + 1) * BL],
                        psU[64:80, :], LORA_SC)
                ve_u[jj] = step("vector", [(PE, pe_u)], u_e, inc=VE)

                sc_xn[jj] = step("scalar", [(VE, ve_xn[jj])],
                                 lambda q, jj=jj, xn=xn: q.dma_start(
                                     out=xn_scr[jj].rearrange("a p t -> p a t"),
                                     in_=xn[:].rearrange("p (a t) -> p a t", t=BL)),
                                 inc=XNS[jj], k=16)

        p1_scope.close()
        barrier()

    if phases >= 2:
        att_scope = ExitStack()
        _aec = att_scope.enter_context
        K8 = _aec(nc.sbuf_tensor("K8", [128, HG * TKV], BF16))
        Vk8 = _aec(nc.sbuf_tensor("Vk8", [128, HG * TKV], BF16))
        wbuf = _aec(nc.sbuf_tensor("wbuf", [128, HG * D], BF16))
        qB_b = _aec(nc.sbuf_tensor("qB_b", [16, D], BF16))
        kB_b = _aec(nc.sbuf_tensor("kB_b", [16, D], BF16))
        vB_b = _aec(nc.sbuf_tensor("vB_b", [16, D], BF16))
        stB = _aec(nc.sbuf_tensor("stB", [16, D], F32))
        masks_s = _aec(nc.sbuf_tensor("masks_s", [128, 5 * 640], BF16))
        cosb = _aec(nc.sbuf_tensor("cosb", [128, 2 * BL], BF16))
        sinb = _aec(nc.sbuf_tensor("sinb", [128, 2 * BL], BF16))
        kvb0 = _aec(nc.sbuf_tensor("kvb0", [128, BL], BF16))
        kvb1 = _aec(nc.sbuf_tensor("kvb1", [128, BL], BF16))
        rotb = _aec(nc.sbuf_tensor("rotb", [128, BL], BF16))
        ro1 = _aec(nc.sbuf_tensor("ro1", [128, BL], BF16))
        Qh8 = _aec(nc.sbuf_tensor("Qh8", [128, HG * TOK], BF16))
        Pb0 = _aec(nc.sbuf_tensor("Pb0", [128, 640], BF16))
        Pb1 = _aec(nc.sbuf_tensor("Pb1", [128, 640], BF16))
        Pbn0 = _aec(nc.sbuf_tensor("Pbn0", [128, 640], BF16))
        Pbn1 = _aec(nc.sbuf_tensor("Pbn1", [128, 640], BF16))
        PT0 = _aec(nc.sbuf_tensor("PT0", [128, 640], BF16))
        PT1 = _aec(nc.sbuf_tensor("PT1", [128, 640], BF16))
        ath0 = _aec(nc.sbuf_tensor("ath0", [128, BL], BF16))
        ath1 = _aec(nc.sbuf_tensor("ath1", [128, BL], BF16))
        statm = _aec(nc.sbuf_tensor("statm", [128, 16], F32))
        if True:

            kvb = [kvb0, kvb1]
            Pb = [Pb0, Pb1]
            Pbn = [Pbn0, Pbn1]
            PTs = [PT0, PT1]
            ath = [ath0, ath1]

            ld_m = step("sync", [], lambda q: q.dma_start(
                out=masks_s[:].rearrange("p (m c) -> p m c", c=640),
                in_=masks_d.rearrange("m p c -> p m c")), inc=ML, k=16)
            for sb, dr, scale in [(qB_b, qBT_d, INV_SQ_DH), (kB_b, kBT_d, None),
                                  (vB_b, vBT_d, None)]:
                ldb = step("sync", [(VE, VE.n)],
                           lambda q, dr=dr: q.dma_start(out=stB[:], in_=dr[:]),
                           inc=LBL, k=16)
                step("vector", [(LBL, ldb)],
                     (lambda q, sb=sb, scale=scale:
                      q.tensor_scalar_mul(sb[:], stB[:], scale) if scale
                      else q.tensor_copy(sb[:], stB[:])),
                     inc=VE)
            ve_lorab = VE.n

            def emit_rope(q, srcb, cslot, dst_ap):
                q.tensor_copy(rotb[0:64, :], srcb[64:128, :])
                q.tensor_copy(rotb[64:128, :], srcb[0:64, :])
                q.tensor_mul(ro1[:], srcb[:], cosb[:, cslot * BL:(cslot + 1) * BL])
                q.tensor_mul(rotb[:], rotb[:], sinb[:, cslot * BL:(cslot + 1) * BL])
                return q.tensor_add(dst_ap, ro1[:], rotb[:])

            sc_cos = {}

            def load_costab(jj, prev_rope_ve):
                def b(q, jj=jj):
                    sl = jj % 2
                    q.dma_start(out=cosb[:, sl * BL:(sl + 1) * BL],
                                in_=cos_d[:, jj * BL:(jj + 1) * BL]).then_inc(COS[jj % 2].h, 16)
                    return q.dma_start(out=sinb[:, sl * BL:(sl + 1) * BL],
                                       in_=sin_d[:, jj * BL:(jj + 1) * BL])
                COS[jj % 2].n += 16
                return step("scalar", [(VE, prev_rope_ve)], b, inc=COS[jj % 2], k=16)

            pe_wbuf_free = 0
            pe_k8_free = 0

            wst_free = [0, 0]

            def stream_strips(wT_src, fold_scale, grp):
                folds = []
                for h in range(HG):
                    hh = grp * HG + h
                    sl = h % 2
                    ldv = step("sync",
                               [(VE, wst_free[sl])],
                               lambda q, hh=hh, sl=sl, wT_src=wT_src: q.dma_start(
                                   out=wst[sl][:].rearrange("p (a o) -> p a o", o=128),
                                   in_=wT_src[:, hh * 128:(hh + 1) * 128].rearrange(
                                       "(a p) o -> p a o", p=128)),
                               inc=WS[sl], k=16)

                    def fold(q, h=h, sl=sl, fold_scale=fold_scale):
                        last = None
                        for a in range(DT):
                            last = q.tensor_scalar_mul(
                                wbuf[:, h * D + a * 128:h * D + (a + 1) * 128],
                                wst[sl][:, a * 128:(a + 1) * 128],
                                fold_scale[:, a:a + 1])
                        return last
                    f = step("vector",
                             [(WS[sl], ldv), (PE, pe_wbuf_free if h < 2 else 0)],
                             fold, inc=VE)
                    folds.append(f)
                    wst_free[sl] = f
                return folds

            rope_prev = 0
            pe_qh8_free = 0
            qh_ready = 0
            ac_ao = {}
            sc_at = {}
            for grp in range(NGRP):
                kvps = ExitStack()
                psKV0 = kvps.enter_context(nc.psum_tensor(f"psKV0_{grp}", [128, 512], F32))
                psKV1 = kvps.enter_context(nc.psum_tensor(f"psKV1_{grp}", [128, 512], F32))
                psKV = [psKV0, psKV1]

                folds = stream_strips(wkT_d, nwT, grp)
                pe_mm = {}
                ac_ev = {}
                ve_rp = {}
                ld_xn = {}
                pe_prev_part = PE.n
                ac_prev_part = AC.n
                for jj in range(NKB):
                    ld_xn[jj] = step("sync",
                                     [(XNS[jj], sc_xn[jj]),
                                      (PE, pe_mm.get((jj - 2) * HG + HG - 1,
                                                     pe_prev_part if jj < 2 else 0))],
                                     lambda q, jj=jj: q.dma_start(
                                         out=xnb[jj % 2][:].rearrange("p (a t) -> p a t", t=BL),
                                         in_=xn_scr[jj].rearrange("a p t -> p a t")),
                                     inc=XNB[jj % 2], k=16)
                    cs = load_costab(jj, rope_prev)
                    for h in range(HG):
                        hh = grp * HG + h
                        i = jj * HG + h
                        sl = i % 2

                        def kmm(q, jj=jj, h=h, hh=hh, sl=sl):
                            for a in range(DT):
                                q.matmul(psKV[sl][:, :],
                                         wbuf[:, h * D + a * 128:h * D + (a + 1) * 128],
                                         xnb[jj % 2][:, a * BL:(a + 1) * BL],
                                         start=(a == 0), stop=False)
                            return q.matmul(psKV[sl][:, :],
                                            kB_b[0:16, hh * 128:(hh + 1) * 128],
                                            u_sb[0:16, TKV + jj * BL:TKV + (jj + 1) * BL],
                                            start=False, stop=True)
                        pe_mm[i] = step("tensor",
                                        [(XNB[jj % 2], ld_xn[jj]), (VE, folds[h]),
                                         (VE, ve_lorab),
                                         (AC, ac_ev.get(i - 2, ac_prev_part))],
                                        kmm, inc=PE)
                        ac_ev[i] = step("scalar",
                                        [(PE, pe_mm[i]), (VE, ve_rp.get(i - 2, 0))],
                                        lambda q, sl=sl: q.activation(
                                            kvb[sl][:], psKV[sl][:, :], AF.Copy),
                                        inc=AC)

                        def krope(q, sl=sl, jj=jj, h=h):
                            return emit_rope(q, kvb[sl], jj % 2,
                                             K8[:, h * TKV + jj * BL:h * TKV + (jj + 1) * BL])
                        wk_ = [(AC, ac_ev[i]), (COS[jj % 2], cs)]
                        if grp > 0 and i == 0:
                            wk_.append((PE, pe_k8_free))
                        ve_rp[i] = step("vector", wk_, krope, inc=VE)
                        rope_prev = ve_rp[i]
                pe_wbuf_free = PE.n

                a1v_ps = ExitStack()
                psVT = [a1v_ps.enter_context(
                    nc.psum_tensor(f"psVT{x}_{grp}", [128, 256], BF16))
                    for x in range(4)]
                ve_part0 = VE.n
                ac_part0 = AC.n
                folds = stream_strips(wvT_d, nwT, grp)
                pe_mm2 = {}
                ac_ev2 = {}
                pe_tr = {}
                ve_vte = {}
                pe_prev_part = PE.n
                ac_prev_part = AC.n
                for jj in range(NKB):
                    ld_xn[jj] = step("sync",
                                     [(XNS[jj], sc_xn[jj]),
                                      (PE, pe_mm2.get((jj - 2) * HG + HG - 1,
                                                      pe_prev_part if jj < 2 else 0))],
                                     lambda q, jj=jj: q.dma_start(
                                         out=xnb[jj % 2][:].rearrange("p (a t) -> p a t", t=BL),
                                         in_=xn_scr[jj].rearrange("a p t -> p a t")),
                                     inc=XNB[jj % 2], k=16)
                    for h in range(HG):
                        hh = grp * HG + h
                        i = jj * HG + h
                        sl = i % 2

                        def vmm(q, jj=jj, h=h, hh=hh, sl=sl):
                            for a in range(DT):
                                q.matmul(psKV[sl][:, :],
                                         wbuf[:, h * D + a * 128:h * D + (a + 1) * 128],
                                         xnb[jj % 2][:, a * BL:(a + 1) * BL],
                                         start=(a == 0), stop=False)
                            return q.matmul(psKV[sl][:, :],
                                            vB_b[0:16, hh * 128:(hh + 1) * 128],
                                            u_sb[0:16, 2 * TKV + jj * BL:2 * TKV + (jj + 1) * BL],
                                            start=False, stop=True)
                        pe_mm2[i] = step("tensor",
                                         [(XNB[jj % 2], ld_xn[jj]), (VE, folds[h]),
                                          (VE, ve_lorab),
                                          (AC, ac_ev2.get(i - 2, ac_prev_part))],
                                         vmm, inc=PE)
                        ac_ev2[i] = step("scalar",
                                         [(PE, pe_mm2[i]), (PE, pe_tr.get(i - 2, 0))],
                                         lambda q, sl=sl: q.activation(
                                             kvb[sl][:], psKV[sl][:, :], AF.Copy),
                                         inc=AC)

                        def vtr(q, sl=sl, i=i):
                            base0 = 2 * (i % 2)
                            last = None
                            for qt in range(4):
                                last = q.transpose(
                                    psVT[base0 + qt % 2][:, (qt // 2) * 128:(qt // 2 + 1) * 128],
                                    kvb[sl][:, qt * 128:(qt + 1) * 128], idb[:])
                            return last
                        wv_ = [(AC, ac_ev2[i]), (VE, ve_vte.get(i - 2, 0))]
                        if i == 0:
                            wv_ += [(VE, ve_part0), (AC, ac_part0)]
                        pe_tr[i] = step("tensor", wv_, vtr, inc=PE)

                        def vte(q, jj=jj, h=h, i=i):
                            base0 = 2 * (i % 2)
                            base = h * TKV + jj * 4 * 128
                            q.tensor_copy(Vk8[:, base:base + 128], psVT[base0][:, 0:128])
                            q.tensor_copy(Vk8[:, base + 256:base + 384],
                                          psVT[base0][:, 128:256])
                            q.tensor_copy(Vk8[:, base + 128:base + 256],
                                          psVT[base0 + 1][:, 0:128])
                            return q.tensor_copy(Vk8[:, base + 384:base + 512],
                                                 psVT[base0 + 1][:, 128:256])
                        wv2_ = [(PE, pe_tr[i])]
                        if grp > 0 and i == 0:
                            wv2_.append((PE, pe_k8_free))
                        ve_vte[i] = step("vector", wv2_, vte, inc=VE)
                pe_wbuf_free = PE.n
                a1v_ps.close()

                folds = stream_strips(wqT_d, nwTq, grp)
                pe_mm3 = {}
                ac_ev3 = {}
                ve_rp3 = {}
                pe_prev_part = PE.n
                ac_prev_part = AC.n
                for j in range(NB):
                    jj = j + 1
                    ld_xn[jj] = step("sync",
                                     [(XNS[jj], sc_xn[jj]),
                                      (PE, pe_mm3.get((j - 2) * HG + HG - 1,
                                                      pe_prev_part if j < 2 else 0))],
                                     lambda q, jj=jj: q.dma_start(
                                         out=xnb[jj % 2][:].rearrange("p (a t) -> p a t", t=BL),
                                         in_=xn_scr[jj].rearrange("a p t -> p a t")),
                                     inc=XNB[jj % 2], k=16)
                    cs = load_costab(jj, rope_prev)
                    for h in range(HG):
                        hh = grp * HG + h
                        i = j * HG + h
                        sl = i % 2

                        def qmm(q, jj=jj, h=h, hh=hh, sl=sl):
                            for a in range(DT):
                                q.matmul(psKV[sl][:, :],
                                         wbuf[:, h * D + a * 128:h * D + (a + 1) * 128],
                                         xnb[jj % 2][:, a * BL:(a + 1) * BL],
                                         start=(a == 0), stop=False)
                            return q.matmul(psKV[sl][:, :],
                                            qB_b[0:16, hh * 128:(hh + 1) * 128],
                                            u_sb[0:16, jj * BL:(jj + 1) * BL],
                                            start=False, stop=True)
                        pe_mm3[i] = step("tensor",
                                         [(XNB[jj % 2], ld_xn[jj]), (VE, folds[h]),
                                          (VE, ve_lorab),
                                          (AC, ac_ev3.get(i - 2, ac_prev_part))],
                                         qmm, inc=PE)
                        ac_ev3[i] = step("scalar",
                                         [(PE, pe_mm3[i]), (VE, ve_rp3.get(i - 2, 0))],
                                         lambda q, sl=sl: q.activation(
                                             kvb[sl][:], psKV[sl][:, :], AF.Copy),
                                         inc=AC)

                        def qrope(q, sl=sl, jj=jj, h=h, j=j):
                            return emit_rope(q, kvb[sl], jj % 2,
                                             Qh8[:, h * TOK + j * BL:h * TOK + (j + 1) * BL])
                        wq_ = [(AC, ac_ev3[i]), (COS[jj % 2], cs)]
                        if grp > 0 and i == 0:
                            wq_.append((PE, pe_qh8_free))
                        ve_rp3[i] = step("vector", wq_, qrope, inc=VE)
                        rope_prev = ve_rp3[i]
                pe_wbuf_free = PE.n
                qh_ready = VE.n
                kvps.close()

                if phases >= 3:
                    a2_ps = ExitStack()
                    psS = [a2_ps.enter_context(
                        nc.psum_tensor(f"psS{x}_{grp}", [128, 640], F32))
                        for x in range(2)]
                    psPT = [a2_ps.enter_context(
                        nc.psum_tensor(f"psPT{x}_{grp}", [128, 640], BF16))
                        for x in range(2)]
                    psAVO = [a2_ps.enter_context(
                        nc.psum_tensor(f"psAVO{x}_{grp}", [128, 512], F32))
                        for x in range(2)]

                    NU = NB * HG * 4
                    pe_smm = {}
                    ve_mx = {}
                    ac_ex = {}
                    ve_nm = {}
                    pe_t2 = {}
                    ve_pt = {}
                    pe_av = {}
                    ac_part0 = AC.n

                    def uidx(u):
                        j, r = divmod(u, HG * 4)
                        h, qt = divmod(r, 4)
                        return j, h, qt

                    def emit_smm(u):
                        j, h, qt = uidx(u)
                        par = u % 2
                        kb = j * 512 + qt * 128
                        mi = qt if j == 0 else 4

                        def smm(q, j=j, h=h, qt=qt, kb=kb, mi=mi, par=par):
                            S = psS[par]
                            qsl = Qh8[:, h * TOK + j * BL + qt * 128:
                                      h * TOK + j * BL + (qt + 1) * 128]
                            q.matmul(S[:, 0:512], qsl,
                                     K8[:, h * TKV + kb:h * TKV + kb + 512],
                                     start=True, stop=False)
                            q.matmul(S[:, 512:640], qsl,
                                     K8[:, h * TKV + kb + 512:h * TKV + kb + 640],
                                     start=True, stop=False)
                            q.matmul(S[:, 0:512], idb[:],
                                     masks_s[:, mi * 640:mi * 640 + 512],
                                     start=False, stop=True)
                            return q.matmul(S[:, 512:640], idb[:],
                                            masks_s[:, mi * 640 + 512:(mi + 1) * 640],
                                            start=False, stop=True)
                        pe_smm[u] = step("tensor",
                                         [(ML, ld_m), (VE, qh_ready),
                                          (AC, ac_ex.get(u - 2, ac_part0))],
                                         smm, inc=PE)

                    def emit_mx(u):
                        par = u % 2
                        ve_mx[u] = step("vector",
                                        [(PE, pe_smm[u]), (AC, ac_ex.get(u - 2, 0))],
                                        lambda q, par=par: q.reduce_max(
                                            statm[:, par:par + 1],
                                            psS[par][:, 0:640], axis=AX.X,
                                            negate=True),
                                        inc=VE)

                    def emit_ex(u):
                        par = u % 2
                        ac_ex[u] = step("scalar",
                                        [(VE, ve_mx[u]), (VE, ve_nm.get(u - 2, 0))],
                                        lambda q, par=par: q.activation(
                                            Pb[par][:], psS[par][:, 0:640], AF.Exp,
                                            bias=statm[:, par:par + 1],
                                            accum_out=statm[:, 2 + par:3 + par]),
                                        inc=AC)

                    def emit_nm(u):
                        par = u % 2
                        rcv = step("vector", [(AC, ac_ex[u])],
                                   lambda q, par=par: q.reciprocal(
                                       statm[:, 4 + par:5 + par],
                                       statm[:, 2 + par:3 + par]),
                                   inc=VE)
                        ve_nm[u] = step("vector", [(PE, pe_t2.get(u - 2, 0)),
                                                   (VE, rcv)],
                                        lambda q, par=par: q.tensor_scalar_mul(
                                            Pbn[par][:], Pb[par][:],
                                            statm[:, 4 + par:5 + par]),
                                        inc=VE)

                    def emit_t2(u):
                        par = u % 2

                        def t2(q, par=par):
                            last = None
                            for c in range(5):
                                last = q.transpose(
                                    psPT[par][:, c * 128:(c + 1) * 128],
                                    Pbn[par][:, c * 128:(c + 1) * 128], idb[:])
                            return last
                        pe_t2[u] = step("tensor",
                                        [(VE, ve_nm[u]), (VE, ve_pt.get(u - 2, 0))],
                                        t2, inc=PE)

                    def emit_pt(u):
                        par = u % 2
                        ve_pt[u] = step("vector",
                                        [(PE, pe_t2[u]), (PE, pe_av.get(u - 2, 0))],
                                        lambda q, par=par: q.tensor_copy(
                                            PTs[par][:], psPT[par][:]),
                                        inc=VE)

                    def emit_av(u):
                        j, h, qt = uidx(u)
                        par = u % 2
                        ih = j * HG + h
                        ihg = grp * NB * HG + ih

                        def avm(q, j=j, h=h, qt=qt, par=par, ih=ih):
                            last = None
                            for i5 in range(5):
                                kt = j * 4 + qt + i5
                                last = q.matmul(
                                    psAVO[ih % 2][:, qt * 128:(qt + 1) * 128],
                                    Vk8[:, h * TKV + kt * 128:h * TKV + (kt + 1) * 128],
                                    PTs[par][:, i5 * 128:(i5 + 1) * 128],
                                    start=(i5 == 0), stop=(i5 == 4))
                            return last
                        pe_av[u] = step("tensor",
                                        [(VE, ve_pt[u]),
                                         (AC, ac_ao.get(ihg - 2, ac_part0))],
                                        avm, inc=PE)
                        if qt == 3:
                            ac_ao[ihg] = step("scalar",
                                              [(PE, pe_av[u]),
                                               (ATH[ihg % 2], sc_at.get(ihg - 2, 0))],
                                              lambda q, ih=ih: q.activation(
                                                  ath[ih % 2][:], psAVO[ih % 2][:, :],
                                                  AF.Copy),
                                              inc=AC)
                            hh = grp * HG + h
                            sc_at[ihg] = step("scalar", [(AC, ac_ao[ihg])],
                                              lambda q, j=j, hh=hh, ih=ih: q.dma_start(
                                                  out=attn_scr[j, hh], in_=ath[ih % 2][:]),
                                              inc=ATH[ihg % 2], k=16)

                    for k in range(NU + 3):
                        if k < NU:
                            emit_smm(k)
                            emit_mx(k)
                            emit_ex(k)
                            emit_nm(k)
                        if 2 <= k < NU + 2:
                            emit_t2(k - 2)
                            emit_pt(k - 2)
                        if k >= 3:
                            emit_av(k - 3)
                    pe_k8_free = PE.n
                    pe_qh8_free = PE.n
                    a2_ps.close()

            sc_attn_all = [(ATH[0], ATH[0].n), (ATH[1], ATH[1].n)]

        att_scope.close()
        barrier()

    pa_scope.close()

    sc_x1 = [0] * NB
    if phases >= 4:
        c_scope = ExitStack()
        _aec = c_scope.enter_context
        wobuf = _aec(nc.sbuf_tensor("wobuf", [128, DT * D], BF16))
        atin = _aec(nc.sbuf_tensor("atin", [128, NH * BL], BF16))
        xcb0 = _aec(nc.sbuf_tensor("xcb0", [128, DT * BL], F32))
        xcb1 = _aec(nc.sbuf_tensor("xcb1", [128, DT * BL], F32))
        oA_b = _aec(nc.sbuf_tensor("oA_b", [128, DT * 16], BF16))
        oB_b = _aec(nc.sbuf_tensor("oB_b", [16, D], BF16))
        stoa = _aec(nc.sbuf_tensor("stoa", [128, DT * 16], F32))
        stB2 = _aec(nc.sbuf_tensor("stB2", [16, D], F32))
        uo_sb = _aec(nc.sbuf_tensor("uo_sb", [16, BL], BF16))
        tmpc = _aec(nc.sbuf_tensor("tmpc", [128, BL], F32))
        rsl = _aec(nc.sbuf_tensor("rsl", [1, 2 * BL], F32))
        psO0 = _aec(nc.psum_tensor("psO0", [128, 512], F32))
        psO1 = _aec(nc.psum_tensor("psO1", [128, 512], F32))
        psUo = _aec(nc.psum_tensor("psUo", [16, 512], F32))
        psBcR = _aec(nc.psum_tensor("psBcR", [128, 512], F32))
        psBcM = _aec(nc.psum_tensor("psBcM", [128, 512], F32))
        if True:

            psO = [psO0, psO1]
            xcb = [xcb0, xcb1]

            ldo = step("sync", [], lambda q: q.dma_start(
                out=stoa[:].rearrange("p (a r) -> p a r", r=16),
                in_=oAT_d.rearrange("(a p) r -> p a r", p=128)), inc=OAL, k=16)
            ldo2 = step("sync", [], lambda q: q.dma_start(out=stB2[:], in_=oBT_d[:]),
                        inc=OAL, k=16)

            def ocast(q):
                q.tensor_copy(oA_b[:], stoa[:])
                return q.tensor_copy(oB_b[:], stB2[:])
            ve_oc = step("vector", [(OAL, ldo2)], ocast, inc=VE)

            wo_folds = []
            for dt_ in range(DT):
                ldv = step("sync",
                           [(VE, wo_folds[dt_ - 2] if dt_ >= 2 else 0)],
                           lambda q, dt_=dt_: q.dma_start(
                               out=wst[dt_ % 2][:].rearrange("p (a o) -> p a o", o=128),
                               in_=woT_d[:, dt_ * 128:(dt_ + 1) * 128].rearrange(
                                   "(a p) o -> p a o", p=128)),
                           inc=WS[dt_ % 2], k=16)

                def wof(q, dt_=dt_):
                    last = None
                    for a in range(DT):
                        last = q.tensor_copy(
                            wobuf[:, dt_ * D + a * 128:dt_ * D + (a + 1) * 128],
                            wst[dt_ % 2][:, a * 128:(a + 1) * 128])
                    return last
                wo_folds.append(step("vector", [(WS[dt_ % 2], ldv)], wof, inc=VE))

            pe_cblk = [0] * NB
            ve_res = [0] * NB
            pe_bcr_prev = 0
            for j in range(NB):
                ld_at = step("sync",
                             sc_attn_all + [(PE, pe_cblk[j - 1] if j >= 1 else 0)],
                             lambda q, j=j: q.dma_start(
                                 out=atin[:].rearrange("p (h t) -> p h t", t=BL),
                                 in_=attn_scr[j].rearrange("h p t -> p h t")),
                             inc=ATL, k=16)
                ld_xc = step("sync", [(X1S[j % 2], sc_x1[j - 2] if j >= 2 else 0)],
                             lambda q, j=j: q.dma_start(
                                 out=xcb[j % 2][:].rearrange("p (a t) -> p a t", t=BL),
                                 in_=xe[:, :, (j + 1) * BL:(j + 2) * BL].rearrange(
                                     "a p t -> p a t")),
                             inc=XC[j % 2], k=16)
                def rw_b(q, j=j):
                    q.dma_start(out=rsl[0:1, 0:BL],
                                in_=row_scr[0, j + 1].unsqueeze(0)).then_inc(RWL.h, 16)
                    return q.dma_start(out=rsl[0:1, BL:2 * BL],
                                       in_=row_scr[1, j + 1].unsqueeze(0))
                RWL.n += 16
                ld_rw = step("sync",
                             [(RWS, sc_row), (PE, pe_bcr_prev)],
                             rw_b, inc=RWL, k=16)

                def uomm(q):
                    last = None
                    for a in range(DT):
                        last = q.matmul(psUo[:, :], oA_b[:, a * 16:(a + 1) * 16],
                                        atin[:, a * BL:(a + 1) * BL],
                                        start=(a == 0), stop=(a == DT - 1))
                    return last
                pe_uo = step("tensor",
                             [(ATL, ld_at), (VE, ve_oc),
                              (VE, ve_res[j - 1] if j >= 1 else 0)],
                             uomm, inc=PE)
                ve_uo = step("vector", [(PE, pe_uo)],
                             lambda q: q.tensor_scalar_mul(uo_sb[:], psUo[:, :], LORA_SC),
                             inc=VE)

                def bcr(q):
                    q.matmul(psBcR[:, :], onerf[:], rsl[0:1, 0:BL], start=True, stop=True)
                    return q.matmul(psBcM[:, :], onerf[:], rsl[0:1, BL:2 * BL],
                                    start=True, stop=True)
                pe_bcr = step("tensor",
                              [(RWL, ld_rw), (VE, ve_res[j - 1] if j >= 1 else 0)],
                              bcr, inc=PE)
                pe_bcr_prev = pe_bcr

                ve_dt = {}
                for dt_ in range(DT):
                    def omm(q, dt_=dt_):
                        for a in range(DT):
                            q.matmul(psO[dt_ % 2][:, :],
                                     wobuf[:, dt_ * D + a * 128:dt_ * D + (a + 1) * 128],
                                     atin[:, a * BL:(a + 1) * BL],
                                     start=(a == 0), stop=False)
                        return q.matmul(psO[dt_ % 2][:, :],
                                        oB_b[0:16, dt_ * 128:(dt_ + 1) * 128],
                                        uo_sb[:], start=False, stop=True)
                    pe_o = step("tensor",
                                [(VE, ve_uo),
                                 (VE, wo_folds[dt_] if j == 0 else 0),
                                 (VE, ve_dt.get(dt_ - 2, 0))],
                                omm, inc=PE)

                    def res(q, dt_=dt_, j=j):
                        xs = xcb[j % 2][:, dt_ * BL:(dt_ + 1) * BL]
                        q.tensor_mul(tmpc[:], xs, psBcR[:, :])
                        q.tensor_add(tmpc[:], tmpc[:], psBcM[:, :])
                        return q.tensor_add(xs, tmpc[:], psO[dt_ % 2][:, :])
                    ve_dt[dt_] = step("vector",
                                      [(PE, pe_o), (PE, pe_bcr), (XC[j % 2], ld_xc)],
                                      res, inc=VE)
                ve_res[j] = ve_dt[DT - 1]
                pe_cblk[j] = PE.n
                sc_x1[j] = step("scalar", [(VE, ve_res[j])],
                                lambda q, j=j: q.dma_start(
                                    out=x1_scr[j].rearrange("a p t -> p a t"),
                                    in_=xcb[j % 2][:].rearrange("p (a t) -> p a t", t=BL)),
                                inc=X1S[j % 2], k=16)

        c_scope.close()
        barrier()

    ws_scope.close()

    if phases >= 5:
        ffn_scope = ExitStack()
        _aec = ffn_scope.enter_context
        ps0 = _aec(nc.psum_tensor("fps0", [128, 512], F32))
        ps1 = _aec(nc.psum_tensor("fps1", [128, 512], F32))
        ps2 = _aec(nc.psum_tensor("fps2", [128, 512], F32))
        ps3 = _aec(nc.psum_tensor("fps3", [128, 512], F32))
        ps4 = _aec(nc.psum_tensor("fps4", [128, 512], F32))
        ps5 = _aec(nc.psum_tensor("fps5", [128, 512], F32))
        psTr0 = _aec(nc.psum_tensor("fpsTr0", [128, 1024], BF16))
        psTr1 = _aec(nc.psum_tensor("fpsTr1", [128, 1024], BF16))
        psTr = [psTr0, psTr1]
        stu_s = _aec(nc.sbuf_tensor("stu_s", [128, 128], F32))
        ecap_s = _aec(nc.sbuf_tensor("ecap_s", [128, 16], F32))
        eidx_s = _aec(nc.sbuf_tensor("eidx_s", [128, 16], F32))
        si4_s = _aec(nc.sbuf_tensor("si4_s", [128, 4], F32))
        io1_s = _aec(nc.sbuf_tensor("io1_s", [128, 1], F32))
        runr = _aec(nc.sbuf_tensor("runr", [1, 32], F32))
        cbc = _aec(nc.sbuf_tensor("cbc", [128, 16], F32))
        t16_r = _aec(nc.sbuf_tensor("t16_r", [128, 16], F32))

        def cstld(q):
            q.dma_start(out=stu_s[:], in_=stu_d[:]).then_inc(CSTL.h, 16)
            q.dma_start(out=ecap_s[:], in_=ecap_d[:]).then_inc(CSTL.h, 16)
            q.dma_start(out=eidx_s[:], in_=eidx_d[:]).then_inc(CSTL.h, 16)
            q.dma_start(out=si4_s[:], in_=si4_d[:]).then_inc(CSTL.h, 16)
            return q.dma_start(out=io1_s[:], in_=io1_d[:])
        CSTL.n += 16 * 4
        ld_cst = step("sync", [], cstld, inc=CSTL, k=16)
        ve_run0 = step("vector", [], lambda q: q.memset(runr[:], 0.0), inc=VE)

        s1_sb = ExitStack()
        _s1 = s1_sb.enter_context
        ybuf = _s1(nc.sbuf_tensor("ybuf", [128, DT * 1024], F32))
        xn2 = _s1(nc.sbuf_tensor("xn2", [128, DT * 1024], BF16))
        rows2 = _s1(nc.sbuf_tensor("rows2", [1, 8 * BL], F32))
        fc0 = _s1(nc.sbuf_tensor("fcst0", [128, BL], BF16))
        fc1 = _s1(nc.sbuf_tensor("fcst1", [128, BL], BF16))
        fs0 = _s1(nc.sbuf_tensor("fsq0", [128, BL], BF16))
        fs1 = _s1(nc.sbuf_tensor("fsq1", [128, BL], BF16))
        ftmp = _s1(nc.sbuf_tensor("ftmp", [128, BL], F32))
        zr_sb = _s1(nc.sbuf_tensor("zr_sb", [16, BL], F32))
        z_sb = _s1(nc.sbuf_tensor("z_sb", [16, BL], F32))
        zt_sb = _s1(nc.sbuf_tensor("zt_sb", [128, 64], F32))
        m1_sb = _s1(nc.sbuf_tensor("m1_sb", [128, 64], F32))
        m2_sb = _s1(nc.sbuf_tensor("m2_sb", [128, 64], F32))
        v12 = _s1(nc.sbuf_tensor("v12", [128, 16], F32))
        m1b_sb = _s1(nc.sbuf_tensor("m1b_sb", [128, 16], F32))
        m2b_sb = _s1(nc.sbuf_tensor("m2b_sb", [128, 16], F32))
        m12_sb = _s1(nc.sbuf_tensor("m12_sb", [128, 16], F32))
        c1_sb = _s1(nc.sbuf_tensor("c1_sb", [128, 16], F32))
        of_sb = _s1(nc.sbuf_tensor("of_sb", [128, 16], F32))
        t16_sb = _s1(nc.sbuf_tensor("t16_sb", [128, 16], F32))
        sv_sb = _s1(nc.sbuf_tensor("sv_sb", [128, 8], F32))
        s1i_sb = _s1(nc.sbuf_tensor("s1i_sb", [128, 1], mybir.dt.int32))
        s2i_sb = _s1(nc.sbuf_tensor("s2i_sb", [128, 1], mybir.dt.int32))
        R0 = _s1(nc.sbuf_tensor("Rt0", [128, ROWW], BF16))
        R1 = _s1(nc.sbuf_tensor("Rt1", [128, ROWW], BF16))
        Rt = [R0, R1]
        if True:
            fcs = [fc0, fc1]
            fss = [fs0, fs1]
            RM2, RQ2, RT2a, RV2, RC2, RE2 = [i * BL for i in range(6)]

            ve_xn2_prev = 0
            pe_z_prev = 0
            ac_yb_prev = 0
            gd_sc = {}
            ac_rev_h = {}
            sc_xn2st = [0, 0]
            ve_run = ve_run0
            pe_cum_prev = 0
            pe_rtr_last = 0
            gti = -1

            def rz(q):
                q.memset(R0[:], 0.0)
                return q.memset(R1[:], 0.0)
            ve_z0 = step("vector", [], rz, inc=VE)
            gd_z = 0

            def gzero(q):
                for k in range(NSLOT // 128):
                    q.dma_start(out=Gall_d[k * 128:(k + 1) * 128, :],
                                in_=R0[:]).then_inc(GZ.h, 16)
                return q.dma_start(out=Gall_d[NSLOT:NSLOT + 1, :], in_=R0[0:1, :])
            if fstop >= 1:
                GZ.n += 16 * (NSLOT // 128 - 1)
                gd_z = step("gpsimd", [(VE, ve_z0)], gzero, inc=GZ, k=16)
            for hf in range(2):
                ld_y = 0
                for b2 in range(2):
                    ld_y = step("gpsimd",
                                [(X1S[b2], sc_x1[2 * hf + b2]),
                                 (VE, ve_xn2_prev if hf == 1 else 0),
                                 (PE, pe_z_prev if hf == 1 else 0),
                                 (AC, ac_yb_prev if hf == 1 else 0)],
                                lambda q, hf=hf, b2=b2: q.dma_start(
                                    out=ybuf[:].rearrange("p (a t) -> p a t", t=1024)[
                                        :, :, b2 * 512:(b2 + 1) * 512],
                                    in_=x1_scr[2 * hf + b2].rearrange("a p t -> p a t")),
                                inc=GY, k=16)

                ve_xn2_ts = [0, 0]
                for ts in range(2):
                    pe_tile = {}
                    for a in range(DT):
                        sl = a % 2
                        w_ = [(GY, ld_y)]
                        if a >= 2:
                            w_.append((PE, pe_tile[a - 2]))
                        if a == 0 and (ts == 1 or hf == 1):
                            w_.append((VE, ve_xn2_ts[ts - 1] if ts == 1 else VE.n))

                        def fcast(q, a=a, sl=sl, ts=ts):
                            ys = ybuf[:, a * 1024 + ts * 512:a * 1024 + (ts + 1) * 512]
                            q.activation(fcs[sl][:], ys, AF.Copy)
                            return q.activation(fss[sl][:], ys, AF.Square)
                        ac_v = step("scalar", w_, fcast, inc=AC)

                        def fstat(q, a=a, sl=sl):
                            q.matmul(ps0[0:1, :], onecb[:], fcs[sl][:],
                                     start=(a == 0), stop=(a == DT - 1))
                            return q.matmul(ps1[0:1, :], onecb[:], fss[sl][:],
                                            start=(a == 0), stop=(a == DT - 1))
                        pe_tile[a] = step("tensor", [(AC, ac_v)], fstat, inc=PE)
                    pe_fstat = pe_tile[DT - 1]

                    def rstep(eng, build, extra=None):
                        w_ = [(RW, RW.n)]
                        if extra:
                            w_.append(extra)
                        return step(eng, w_, build, inc=RW)

                    rstep("vector", lambda q: q.tensor_scalar_mul(
                        rows2[0:1, RM2:RM2 + BL], ps0[0:1, :], 1.0 / D),
                        extra=(PE, pe_fstat))
                    rstep("vector", lambda q: q.tensor_scalar_mul(
                        rows2[0:1, RQ2:RQ2 + BL], ps1[0:1, :], 1.0 / D))
                    rstep("vector", lambda q: q.tensor_mul(
                        rows2[0:1, RT2a:RT2a + BL], rows2[0:1, RM2:RM2 + BL],
                        rows2[0:1, RM2:RM2 + BL]))
                    rstep("vector", lambda q: q.tensor_sub(
                        rows2[0:1, RV2:RV2 + BL], rows2[0:1, RQ2:RQ2 + BL],
                        rows2[0:1, RT2a:RT2a + BL]))
                    rstep("vector", lambda q: q.tensor_scalar_add(
                        rows2[0:1, RT2a:RT2a + BL], rows2[0:1, RV2:RV2 + BL], EPS_LN))
                    rstep("scalar", lambda q: q.activation(
                        rows2[0:1, RT2a:RT2a + BL], rows2[0:1, RT2a:RT2a + BL],
                        AF.Sqrt))
                    rstep("vector", lambda q: q.reciprocal(
                        rows2[0:1, RC2:RC2 + BL], rows2[0:1, RT2a:RT2a + BL]))
                    rstep("vector", lambda q: q.tensor_mul(
                        rows2[0:1, RE2:RE2 + BL], rows2[0:1, RM2:RM2 + BL],
                        rows2[0:1, RC2:RC2 + BL]))
                    row_c2 = rstep("vector", lambda q: q.tensor_scalar_mul(
                        rows2[0:1, RE2:RE2 + BL], rows2[0:1, RE2:RE2 + BL], -1.0))

                    def zmm(q, ts=ts):
                        last = None
                        for a in range(DT):
                            last = q.matmul(
                                ps2[0:16, :], gw_s[:, a * 16:(a + 1) * 16],
                                ybuf[:, a * 1024 + ts * 512:a * 1024 + (ts + 1) * 512],
                                start=(a == 0), stop=(a == DT - 1))
                        return last
                    pe_z = step("tensor",
                                [(GY, ld_y),
                                 (VE, ve_xn2_ts[ts - 1] if ts == 1 else ve_xn2_prev)],
                                zmm, inc=PE)
                    pe_z_prev = pe_z
                    ve_zr = step("vector", [(PE, pe_z)],
                                 lambda q: q.tensor_copy(zr_sb[:], ps2[0:16, :]),
                                 inc=VE)

                    def zbc(q):
                        q.matmul(ps2[:, :], onerf[:], rows2[0:1, RC2:RC2 + BL],
                                 start=True, stop=True)
                        return q.matmul(ps3[0:16, :], Se_row[:],
                                        rows2[0:1, RE2:RE2 + BL], start=True, stop=True)
                    pe_zbc = step("tensor",
                                  [(RW, row_c2), (VE, ve_zr), (SSE, p0_se_done)],
                                  zbc, inc=PE)

                    rstep("vector", lambda q: q.tensor_mul(
                        z_sb[:], zr_sb[:], ps2[0:16, :]), extra=(PE, pe_zbc))
                    rstep("vector", lambda q: q.tensor_add(
                        z_sb[:], z_sb[:], ps3[0:16, :]))
                    row_zs = rstep("scalar", lambda q: q.activation(
                        z_sb[:], z_sb[:], AF.Sigmoid))

                    def ztr(q):
                        last = None
                        for c in range(4):
                            last = q.matmul(ps4[:, c * 16:(c + 1) * 16],
                                            z_sb[0:16, c * 128:(c + 1) * 128],
                                            id16f[:], start=True, stop=True)
                        return last
                    pe_zt = step("tensor", [(RW, row_zs), (DBG_SEM, DBG_SEM.n)],
                                 ztr, inc=PE)
                    row_ztc = rstep("vector", lambda q: q.tensor_copy(
                        zt_sb[:], ps4[:, 0:64]), extra=(PE, pe_zt))
                    if DBGI:
                        DBG_SEM.n += 0
                        step("scalar", [(RW, row_ztc)],
                             lambda q, hf=hf, ts=ts: q.dma_start(
                                 out=dbgz_d[hf * 2 + ts], in_=zt_sb[:]),
                             inc=DBG_SEM, k=16)

                    zt3 = zt_sb[:].rearrange("p (c e) -> p c e", e=16)
                    m13 = m1_sb[:].rearrange("p (c e) -> p c e", e=16)
                    m23 = m2_sb[:].rearrange("p (c e) -> p c e", e=16)
                    rstep("vector", lambda q: q.reduce_max(v12[:, 0:4], zt3, axis=AX.X))
                    rstep("vector", lambda q: q.tensor_tensor(
                        m13, zt3,
                        v12[:, 0:4].rearrange("p c -> p c ()").broadcast_to([128, 4, 16]),
                        ALU.is_equal))
                    rstep("vector", lambda q: q.tensor_scalar_mul(
                        m2_sb[:], m1_sb[:], NEG_BIG))
                    rstep("vector", lambda q: q.tensor_add(m2_sb[:], m2_sb[:], zt_sb[:]))
                    rstep("vector", lambda q: q.reduce_max(v12[:, 4:8], m23, axis=AX.X))
                    rstep("vector", lambda q: q.tensor_tensor(
                        m23, m23,
                        v12[:, 4:8].rearrange("p c -> p c ()").broadcast_to([128, 4, 16]),
                        ALU.is_equal))
                    rstep("vector", lambda q: q.tensor_add(
                        v12[:, 8:12], v12[:, 0:4], v12[:, 4:8]))
                    rstep("vector", lambda q: q.reciprocal(
                        v12[:, 12:16], v12[:, 8:12]))
                    rstep("vector", lambda q: q.tensor_mul(
                        v12[:, 0:4], v12[:, 0:4], v12[:, 12:16]))
                    rstep("vector", lambda q: q.tensor_mul(
                        v12[:, 4:8], v12[:, 4:8], v12[:, 12:16]))
                    rstep("vector", lambda q: q.tensor_tensor(
                        m13, m13,
                        v12[:, 0:4].rearrange("p c -> p c ()").broadcast_to([128, 4, 16]),
                        ALU.mult))
                    row_m = rstep("vector", lambda q: q.tensor_tensor(
                        m23, m23,
                        v12[:, 4:8].rearrange("p c -> p c ()").broadcast_to([128, 4, 16]),
                        ALU.mult))
                    if DBGI:
                        def dmpv(q, hf=hf, ts=ts):
                            q.dma_start(out=dbgv_d[hf * 2 + ts],
                                        in_=v12[:]).then_inc(DBG_SEM.h, 16)
                            q.dma_start(out=dbgm_d[hf * 2 + ts, 0],
                                        in_=m1_sb[:]).then_inc(DBG_SEM.h, 16)
                            return q.dma_start(out=dbgm_d[hf * 2 + ts, 1],
                                               in_=m2_sb[:])
                        DBG_SEM.n += 32
                        step("scalar", [(RW, row_m)], dmpv, inc=DBG_SEM, k=16)

                    pe_e2 = step("tensor", [(RW, RW.n)],
                                 lambda q: q.matmul(ps0[:, :], onerf[:],
                                                    rows2[0:1, RE2:RE2 + BL],
                                                    start=True, stop=True),
                                 inc=PE)

                    def xn2_b(q, ts=ts):
                        last = None
                        for a in range(DT):
                            ys = ybuf[:, a * 1024 + ts * 512:a * 1024 + (ts + 1) * 512]
                            q.tensor_mul(ftmp[:], ys, ps2[:, :])
                            last = q.tensor_add(
                                xn2[:, a * 1024 + ts * 512:a * 1024 + (ts + 1) * 512],
                                ftmp[:], ps0[:, :])
                        return last
                    w_x = [(PE, pe_e2)]
                    if hf == 1 and ts == 0:
                        w_x += [(PE, pe_rtr_last), (XN2S[0], sc_xn2st[0])]
                    ve_xn2_ts[ts] = step("vector", w_x, xn2_b, inc=VE)

                    for c in range(4 if fstop >= 2 or CHUNK_ON else 0):
                        gti += 1
                        rsl = gti % 2
                        g_base = gti * 128

                        def mbin(q, c=c):
                            return vseq(q, [
                                lambda q: q.tensor_scalar(
                                    m1b_sb[:], m1_sb[:, c * 16:(c + 1) * 16],
                                    0.0, None, ALU.is_gt),
                                lambda q: q.tensor_scalar(
                                    m2b_sb[:], m2_sb[:, c * 16:(c + 1) * 16],
                                    0.0, None, ALU.is_gt),
                                lambda q: q.tensor_add(m12_sb[:], m1b_sb[:],
                                                       m2b_sb[:])])
                        ve_mb = step("vector",
                                     [(RW, row_m), (PE, pe_cum_prev),
                                      (DBG_SEM, DBG_SEM.n)],
                                     mbin, inc=VE)
                        if CHL < 2:
                            continue

                        def cumm(q):
                            q.matmul(ps5[:, 0:16], stu_s[:], m12_sb[:],
                                     start=True, stop=False)
                            q.matmul(ps5[:, 0:16], onerf[:], runr[0:1, 0:16],
                                     start=False, stop=True)
                            return q.matmul(ps5[0:1, 16:32], onecf[:], m12_sb[:],
                                            start=True, stop=True)
                        pe_cum = step("tensor",
                                      [(VE, ve_mb), (VE, ve_run), (CSTL, ld_cst)],
                                      cumm, inc=PE)
                        pe_cum_prev = pe_cum

                        def crd(q):
                            q.tensor_copy(c1_sb[:], ps5[:, 0:16])
                            return q.tensor_add(runr[0:1, 0:16], runr[0:1, 0:16],
                                                ps5[0:1, 16:32])
                        ve_run = step("vector", [(PE, pe_cum)], crd, inc=VE)
                        if CHL < 3:
                            continue

                        def sbuild(q, gti=gti):
                            return vseq(q, [
                                lambda q: q.tensor_scalar(of_sb[:], c1_sb[:],
                                                          CAP - 0.5, None, ALU.is_ge),
                                lambda q: q.tensor_scalar_mul(of_sb[:], of_sb[:],
                                                              HUGE_OFF),
                                lambda q: q.tensor_add(t16_sb[:], c1_sb[:], ecap_s[:]),
                                lambda q: q.tensor_add(t16_sb[:], t16_sb[:], of_sb[:]),
                                lambda q: q.tensor_mul(of_sb[:], t16_sb[:], m1b_sb[:]),
                                lambda q: q.reduce_sum(sv_sb[:, 0:1], of_sb[:],
                                                       axis=AX.X),
                                lambda q: q.tensor_mul(of_sb[:], t16_sb[:], m2b_sb[:]),
                                lambda q: q.reduce_sum(sv_sb[:, 1:2], of_sb[:],
                                                       axis=AX.X),
                                lambda q: q.tensor_mul(of_sb[:], m1b_sb[:], eidx_s[:]),
                                lambda q: q.reduce_sum(sv_sb[:, 2:3], of_sb[:],
                                                       axis=AX.X),
                                lambda q: q.tensor_scalar(sv_sb[:, 0:1], sv_sb[:, 0:1],
                                                          float(NSLOT + 2 * gti),
                                                          None, ALU.min),
                                lambda q: q.tensor_scalar(sv_sb[:, 1:2], sv_sb[:, 1:2],
                                                          float(NSLOT + 2 * gti + 1),
                                                          None, ALU.min),
                                lambda q: q.tensor_copy(s1i_sb[:], sv_sb[:, 0:1]),
                                lambda q: q.tensor_copy(s2i_sb[:], sv_sb[:, 1:2])])
                        ve_sb = step("vector",
                                     [(VE, ve_run),
                                      (GSC[(gti - 1) % 2], gd_sc.get(gti - 1, 0))],
                                     sbuild, inc=VE)
                        if DBGI:
                            DBG_SEM.n += 16 * 4
                            step("scalar", [(VE, ve_sb)],
                                 lambda q, gti=gti: (
                                     q.dma_start(out=dbgi_d[gti, 0],
                                                 in_=s1i_sb[:]).then_inc(DBG_SEM.h, 16),
                                     q.dma_start(out=dbgf_d[gti, 0],
                                                 in_=m12_sb[:]).then_inc(DBG_SEM.h, 16),
                                     q.dma_start(out=dbgf_d[gti, 1],
                                                 in_=c1_sb[:]).then_inc(DBG_SEM.h, 16),
                                     q.dma_start(out=dbgf_d[gti, 2],
                                                 in_=t16_sb[:]).then_inc(DBG_SEM.h, 16),
                                     q.dma_start(out=dbgi_d[gti, 1],
                                                 in_=s2i_sb[:]))[-1],
                                 inc=DBG_SEM, k=16)
                        if CHL < 4:
                            continue

                        def rtr(q, ts=ts, c=c):
                            last = None
                            for a in range(DT):
                                last = q.transpose(
                                    psTr[a // 8][:, (a % 8) * 128:(a % 8 + 1) * 128],
                                    xn2[:, a * 1024 + ts * 512 + c * 128:
                                        a * 1024 + ts * 512 + (c + 1) * 128],
                                    idb[:])
                            return last
                        pe_rtr = step("tensor",
                                      [(VE, ve_xn2_ts[ts]),
                                       (AC, ac_rev_h.get(gti - 1, 0))],
                                      rtr, inc=PE)
                        pe_rtr_last = pe_rtr

                        def revac(q, rsl=rsl):
                            q.activation(Rt[rsl][:, 0:1024], psTr0[:], AF.Copy)
                            return q.activation(Rt[rsl][:, 1024:2048], psTr1[:],
                                                AF.Copy)
                        ac_rev = step("scalar",
                                      [(PE, pe_rtr),
                                       (GSC[gti % 2] if gti >= 2 else GZ,
                                        gd_sc.get(gti - 2, gd_z if gti < 2 else 0))],
                                      revac, inc=AC)
                        ac_rev_h[gti] = ac_rev
                        if CHL < 5:
                            continue

                        def rtail(q, rsl=rsl, c=c, g_base=g_base):
                            return vseq(q, [
                                lambda q: q.tensor_mul(
                                    of_sb[:], m1_sb[:, c * 16:(c + 1) * 16],
                                    m1b_sb[:]),
                                lambda q: q.reduce_sum(sv_sb[:, 3:4], of_sb[:],
                                                       axis=AX.X),
                                lambda q: q.tensor_mul(
                                    of_sb[:], m2_sb[:, c * 16:(c + 1) * 16],
                                    m2b_sb[:]),
                                lambda q: q.reduce_sum(sv_sb[:, 4:5], of_sb[:],
                                                       axis=AX.X),
                                lambda q: q.tensor_copy(Rt[rsl][:, 2048:2049],
                                                        sv_sb[:, 3:4]),
                                lambda q: q.tensor_copy(Rt[rsl][:, 2049:2050],
                                                        sv_sb[:, 4:5]),
                                lambda q: q.tensor_copy(Rt[rsl][:, 2050:2051],
                                                        sv_sb[:, 2:3]),
                                lambda q: q.memset(Rt[rsl][:, 2051:2052],
                                                   float(g_base // 128)),
                                lambda q: q.tensor_copy(Rt[rsl][:, 2052:2053],
                                                        io1_s[:])])
                        ve_rt = step("vector", [(VE, ve_sb), (AC, ac_rev)],
                                     rtail, inc=VE)

                        def rscat(q, rsl=rsl):
                            q.indirect_dma_start(
                                out=Gall_d[:], out_offset=IOX(s1i_sb[:], 0),
                                in_=Rt[rsl][:], in_offset=None).then_inc(GSC[gti % 2].h, 16)
                            return q.indirect_dma_start(
                                out=Gall_d[:], out_offset=IOX(s2i_sb[:], 0),
                                in_=Rt[rsl][:], in_offset=None)
                        if fstop >= 2 or SCAT_ON:
                            GSC[gti % 2].n += 16
                            gd_sc[gti] = step("gpsimd", [(VE, ve_rt)], rscat,
                                              inc=GSC[gti % 2], k=16)

                ve_xn2_prev = ve_xn2_ts[1]
                ac_yb_prev = AC.n

                sc_xn2st[hf] = step("scalar", [(VE, ve_xn2_prev)],
                                    lambda q, hf=hf: q.dma_start(
                                        out=xn2_scr[hf].rearrange("a p t -> p a t"),
                                        in_=xn2[:].rearrange("p (a t) -> p a t",
                                                             t=1024)),
                                    inc=XN2S[hf], k=16)

            gd_scat_all = [(GSC[0], GSC[0].n), (GSC[1], GSC[1].n)]
            pe_cb = step("tensor", [(VE, ve_run), (PE, pe_cum_prev)],
                         lambda q: q.matmul(ps5[:, 32:48], onerf[:],
                                            runr[0:1, 0:16], start=True, stop=True),
                         inc=PE)
            ve_cb = step("vector", [(PE, pe_cb)],
                         lambda q: q.tensor_copy(cbc[:], ps5[:, 32:48]), inc=VE)

        s1_sb.close()

        rt_sb = ExitStack()
        _rt = rt_sb.enter_context
        Gtok0 = _rt(nc.sbuf_tensor("Gtok0", [128, NT * ROWW], BF16))
        Gtok1 = _rt(nc.sbuf_tensor("Gtok1", [128, NT * ROWW], BF16))
        GfeatA = [_rt(nc.sbuf_tensor(f"Gfeat{x}", [128, DT * CAP], BF16))
                  for x in range(2)]
        w1s = [_rt(nc.sbuf_tensor(f"w1s{x}", [128, DT * 128], BF16))
               for x in range(4)]
        w3s = [_rt(nc.sbuf_tensor(f"w3s{x}", [128, DT * 128], BF16))
               for x in range(4)]
        w2s0 = _rt(nc.sbuf_tensor("w2s0", [128, NIT * 512], BF16))
        w2s1 = _rt(nc.sbuf_tensor("w2s1", [128, NIT * 512], BF16))
        Hbuf = _rt(nc.sbuf_tensor("Hbuf", [128, NIT * CAP], BF16))
        sA0 = _rt(nc.sbuf_tensor("sA0", [128, BL], F32))
        sA1 = _rt(nc.sbuf_tensor("sA1", [128, BL], F32))
        ybf = _rt(nc.sbuf_tensor("ybf", [128, BL], BF16))
        ytk = _rt(nc.sbuf_tensor("ytk", [128, NT * 2048], BF16))
        yscl0 = _rt(nc.sbuf_tensor("yscl0", [128, 2048], F32))
        yscl1 = _rt(nc.sbuf_tensor("yscl1", [128, 2048], F32))
        wsl_sb = _rt(nc.sbuf_tensor("wsl_sb", [128, 16], F32))
        tdiA = [_rt(nc.sbuf_tensor(f"tdi{x}", [128, 1], mybir.dt.int32))
                for x in range(NT)]
        if True:
            s1_end = [(XN2S[0], XN2S[0].n), (XN2S[1], XN2S[1].n),
                      (VE, VE.n), (AC, AC.n)]
            Gtok = [Gtok0, Gtok1]
            w2s = [w2s0, w2s1]
            sAs = [sA0, sA1]
            yscl = [yscl0, yscl1]
            psA = [ps0, ps1]
            psB = [ps2, ps3]
            psY = [ps4, ps5]

            ab_hist = [0, 0, 0, 0]
            y_hist = [0, 0]
            gf_rd = [0, 0]
            ysc_par = [0, 0]
            tr_hist = [0, 0]
            ysc_hist = [0, 0]
            td_hist = [0, 0]
            gtok_rd = [0, 0]
            ve_h_u = {}
            ac_si_u = {}
            ac_ye = {}
            pe_ybf_rd = {}
            unf = 0
            yk = 0
            pe_ab_last = 0
            ve_h_last = 0
            ve_ys_last3 = 0
            for e in range(NEXP if fstop >= 2 else 0):
                esl = e % 2
                ld_G = step("sync",
                            gd_scat_all + (s1_end if e < 2 else []) +
                            [(PE, gtok_rd[esl]), (VE, td_hist[-2])],
                            lambda q, e=e, esl=esl: q.dma_start(
                                out=Gtok[esl][:].rearrange("p (r w) -> p r w", w=ROWW),
                                in_=Gall_d[e * CAP:(e + 1) * CAP, :].rearrange(
                                    "(r p) w -> p r w", p=128)),
                            inc=GL[esl], k=16)

                g3 = Gtok[esl][:].rearrange("p (r w) -> p r w", w=ROWW)

                def tmeta(q, e=e, g3=g3):
                    ops = [
                        lambda q: q.tensor_copy(wsl_sb[:, 0:NT], g3[:, :, 2048]),
                        lambda q: q.tensor_copy(wsl_sb[:, 4:4 + NT], g3[:, :, 2049]),
                        lambda q: q.tensor_copy(wsl_sb[:, 8:8 + NT], g3[:, :, 2050]),
                        lambda q: q.tensor_scalar(t16_r[:, 0:NT], wsl_sb[:, 8:8 + NT],
                                                  float(e), None, ALU.is_equal),
                        lambda q: q.tensor_sub(wsl_sb[:, 0:NT], wsl_sb[:, 0:NT],
                                               wsl_sb[:, 4:4 + NT]),
                        lambda q: q.tensor_mul(wsl_sb[:, 0:NT], wsl_sb[:, 0:NT],
                                               t16_r[:, 0:NT]),
                        lambda q: q.tensor_add(wsl_sb[:, 0:NT], wsl_sb[:, 0:NT],
                                               wsl_sb[:, 4:4 + NT]),
                        lambda q: q.tensor_copy(wsl_sb[:, 12:12 + NT],
                                                g3[:, :, 2051]),
                        lambda q: q.tensor_scalar_mul(wsl_sb[:, 12:12 + NT],
                                                      wsl_sb[:, 12:12 + NT], 128.0),
                        lambda q: q.tensor_copy(t16_r[:, 4:4 + NT], g3[:, :, 2052]),
                        lambda q: q.tensor_add(wsl_sb[:, 12:12 + NT],
                                               wsl_sb[:, 12:12 + NT],
                                               t16_r[:, 4:4 + NT]),
                        lambda q: q.tensor_scalar_mul(t16_r[:, 0:NT], t16_r[:, 0:NT],
                                                      -2048.0),
                        lambda q: q.tensor_scalar_add(t16_r[:, 0:NT], t16_r[:, 0:NT],
                                                      2048.0),
                        lambda q: q.tensor_add(wsl_sb[:, 12:12 + NT],
                                               wsl_sb[:, 12:12 + NT],
                                               t16_r[:, 0:NT]),
                        lambda q: q.tensor_scalar(t16_r[:, 4:4 + NT],
                                                  si4_s[:, 0:NT],
                                                  cbc[:, e:e + 1], None, ALU.is_ge),
                        lambda q: q.tensor_scalar_mul(t16_r[:, 4:4 + NT],
                                                      t16_r[:, 4:4 + NT], HUGE_OFF),
                        lambda q: q.tensor_add(wsl_sb[:, 12:12 + NT],
                                               wsl_sb[:, 12:12 + NT],
                                               t16_r[:, 4:4 + NT]),
                    ]
                    for x in range(NT):
                        ops.append(lambda q, x=x: q.tensor_scalar(
                            t16_r[:, 8 + x:9 + x], wsl_sb[:, 12 + x:13 + x],
                            float(2 * TOK + NT * e + x), None, ALU.min))
                        ops.append(lambda q, x=x: q.tensor_copy(
                            tdiA[x][:], t16_r[:, 8 + x:9 + x]))
                    return vseq(q, ops)
                ve_td = step("vector",
                             [(GL[esl], ld_G), (VE, ve_cb),
                              (GYS[0], GYS[0].n), (GYS[1], GYS[1].n)],
                             tmeta, inc=VE)
                td_hist.append(ve_td)

                for g8 in range(8):
                    tsl = len(tr_hist) % 2

                    def gtr(q, g8=g8, esl=esl, tsl=tsl):
                        last = None
                        for aa in (2 * g8, 2 * g8 + 1):
                            for stl in range(NT):
                                last = q.transpose(
                                    psTr[tsl][:, ((aa % 2) * NT + stl) * 128:
                                              ((aa % 2) * NT + stl + 1) * 128],
                                    Gtok[esl][:, stl * ROWW + aa * 128:
                                              stl * ROWW + (aa + 1) * 128],
                                    idb[:])
                        return last
                    pe_g = step("tensor",
                                [(GL[esl], ld_G), (AC, tr_hist[-2])],
                                gtr, inc=PE)

                    def gev(q, g8=g8, tsl=tsl, esl=esl):
                        return q.activation(
                            GfeatA[esl][:, 2 * g8 * CAP:(2 * g8 + 2) * CAP],
                            psTr[tsl][:, 0:2 * NT * 128], AF.Copy)
                    ac_g = step("scalar", [(PE, pe_g), (PE, gf_rd[esl])],
                                gev, inc=AC)
                    tr_hist.append(ac_g)
                gtok_rd[esl] = PE.n
                ac_gf_done = AC.n

                for it in range(NIT):
                    csl = it % 4

                    def w13ld(q, it=it, csl=csl, e=e):
                        q.dma_start(
                            out=w1s[csl][:].rearrange("p (a i) -> p a i", i=128),
                            in_=w1T_d[e][:, it * 128:(it + 1) * 128].rearrange(
                                "(a p) i -> p a i", p=128)).then_inc(W13[csl].h, 16)
                        return q.dma_start(
                            out=w3s[csl][:].rearrange("p (a i) -> p a i", i=128),
                            in_=w3T_d[e][:, it * 128:(it + 1) * 128].rearrange(
                                "(a p) i -> p a i", p=128))
                    W13[csl].n += 16
                    ld_w13 = step("sync",
                                  [(PE, ab_hist[-4])] +
                                  (s1_end if e == 0 and it < 2 else []),
                                  w13ld, inc=W13[csl], k=16)

                    u = unf
                    unf += 1
                    psl = u % 2

                    def abmm(q, it=it, csl=csl, psl=psl, esl=esl):
                        last = None
                        for a in range(DT):
                            q.matmul(psA[psl][:, 0:CAP],
                                     w1s[csl][:, a * 128:(a + 1) * 128],
                                     GfeatA[esl][:, a * CAP:(a + 1) * CAP],
                                     start=(a == 0), stop=(a == DT - 1))
                        for a in range(DT):
                            last = q.matmul(psB[psl][:, 0:CAP],
                                            w3s[csl][:, a * 128:(a + 1) * 128],
                                            GfeatA[esl][:, a * CAP:(a + 1) * CAP],
                                            start=(a == 0), stop=(a == DT - 1))
                        return last
                    w_ab = [(W13[csl], ld_w13), (AC, ac_gf_done),
                            (AC, ac_si_u.get(u - 2, 0)),
                            (VE, ve_h_u.get(u - 2, 0))]
                    if e == 0 and it == 0:
                        w_ab.append((VE, ve_cb))
                    pe_ab = step("tensor", w_ab, abmm, inc=PE)
                    ab_hist.append(pe_ab)
                    pe_ab_last = pe_ab
                    gf_rd[esl] = pe_ab

                    ac_si_u[u] = step("scalar",
                                      [(PE, pe_ab), (VE, ve_h_u.get(u - 2, 0))],
                                      lambda q, psl=psl: q.activation(
                                          sAs[psl][:, 0:CAP], psA[psl][:, 0:CAP],
                                          AF.Silu),
                                      inc=AC)
                    w_h = [(AC, ac_si_u[u])]
                    if it == 0:
                        w_h.append((PE, y_hist[-1]))
                    ve_h_u[u] = step("vector", w_h,
                                     lambda q, psl=psl, it=it: q.tensor_mul(
                                         Hbuf[:, it * CAP:(it + 1) * CAP],
                                         sAs[psl][:, 0:CAP], psB[psl][:, 0:CAP]),
                                     inc=VE)
                    ve_h_last = ve_h_u[u]

                ac_ytk_last = 0
                for dq in range(4):
                    def w2ld(q, dq=dq, e=e):
                        return q.dma_start(
                            out=w2s[dq % 2][:].rearrange("p (i o) -> p i o", o=512),
                            in_=w2T_d[e][:, dq * 512:(dq + 1) * 512].rearrange(
                                "(i p) o -> p i o", p=128))
                    ld_w2 = step("sync",
                                 [(PE, y_hist[-2])] +
                                 (s1_end if e == 0 and dq < 2 else []),
                                 w2ld, inc=W2L[dq % 2], k=16)
                    for dtl in range(4):
                        dch = dq * 4 + dtl
                        yi = yk
                        yk += 1
                        ysl = yi % 2

                        def ymm(q, dq=dq, dtl=dtl, ysl=ysl):
                            last = None
                            for it in range(NIT):
                                last = q.matmul(
                                    psY[ysl][:, 0:CAP],
                                    w2s[dq % 2][:, it * 512 + dtl * 128:
                                                it * 512 + (dtl + 1) * 128],
                                    Hbuf[:, it * CAP:(it + 1) * CAP],
                                    start=(it == 0), stop=(it == NIT - 1))
                            return last
                        pe_y = step("tensor",
                                    [(W2L[dq % 2], ld_w2), (VE, ve_h_last),
                                     (AC, ac_ye.get(yi - 2, 0))],
                                    ymm, inc=PE)
                        if dtl == 3:
                            y_hist.append(pe_y)

                        ac_ye[yi] = step("scalar",
                                         [(PE, pe_y), (PE, pe_ybf_rd.get(yi - 1, 0))],
                                         lambda q, ysl=ysl: q.activation(
                                             ybf[:, 0:CAP], psY[ysl][:, 0:CAP],
                                             AF.Copy),
                                         inc=AC)

                        tsl = len(tr_hist) % 2

                        def ytr(q, tsl=tsl):
                            last = None
                            for stl in range(NT):
                                last = q.transpose(
                                    psTr[tsl][:, stl * 128:(stl + 1) * 128],
                                    ybf[:, stl * 128:(stl + 1) * 128], idb[:])
                            return last
                        pe_yt = step("tensor", [(AC, ac_ye[yi]), (AC, tr_hist[-2])],
                                     ytr, inc=PE)
                        pe_ybf_rd[yi] = pe_yt

                        def ytev(q, dch=dch, tsl=tsl):
                            return q.activation(
                                ytk[:].rearrange("p (s w) -> p s w", w=2048)[
                                    :, :, dch * 128:(dch + 1) * 128],
                                psTr[tsl][:, 0:NT * 128].rearrange(
                                    "p (s w) -> p s w", w=128),
                                AF.Copy)
                        w_yt = [(PE, pe_yt)]
                        if dch == 0:
                            w_yt.append((VE, ve_ys_last3))
                        ac_yt = step("scalar", w_yt, ytev, inc=AC)
                        tr_hist.append(ac_yt)
                        ac_ytk_last = ac_yt

                for stl in range(NT):
                    ssl = stl % 2

                    def yscale(q, stl=stl, ssl=ssl):
                        return q.tensor_scalar_mul(
                            yscl[ssl][:], ytk[:, stl * 2048:(stl + 1) * 2048],
                            wsl_sb[:, stl:stl + 1])
                    ve_ys = step("vector",
                                 [(AC, ac_ytk_last), (VE, ve_td),
                                  (GYS[ssl], ysc_par[ssl])],
                                 yscale, inc=VE)
                    if stl == NT - 1:
                        ve_ys_last3 = ve_ys

                    def yscat(q, stl=stl, ssl=ssl):
                        return q.indirect_dma_start(
                            out=arena_d[:], out_offset=IOX(tdiA[stl][:], 0),
                            in_=yscl[ssl][:], in_offset=None)
                    gd_ys = step("gpsimd", [(VE, ve_ys)], yscat,
                                 inc=GYS[ssl], k=16)
                    ysc_par[ssl] = gd_ys
                    ysc_hist.append(gd_ys)

            gd_y_all = [(GYS[0], GYS[0].n), (GYS[1], GYS[1].n)]

        rt_sb.close()

        sh_sb = ExitStack()
        _sh = sh_sb.enter_context
        xn2s0 = _sh(nc.sbuf_tensor("xn2s0", [128, DT * 512], BF16))
        xn2s1 = _sh(nc.sbuf_tensor("xn2s1", [128, DT * 512], BF16))
        w1h0 = _sh(nc.sbuf_tensor("w1h0", [128, DT * 128], BF16))
        w1h1 = _sh(nc.sbuf_tensor("w1h1", [128, DT * 128], BF16))
        w3h0 = _sh(nc.sbuf_tensor("w3h0", [128, DT * 128], BF16))
        w3h1 = _sh(nc.sbuf_tensor("w3h1", [128, DT * 128], BF16))
        w2h0 = _sh(nc.sbuf_tensor("w2h0", [128, NIT2 * 512], BF16))
        w2h1 = _sh(nc.sbuf_tensor("w2h1", [128, NIT2 * 512], BF16))
        H2 = _sh(nc.sbuf_tensor("H2", [128, NIT2 * 512], BF16))
        sB0 = _sh(nc.sbuf_tensor("sB0", [128, BL], F32))
        sB1 = _sh(nc.sbuf_tensor("sB1", [128, BL], F32))
        ybf2 = _sh(nc.sbuf_tensor("ybf2", [128, BL], BF16))
        ytkS = _sh(nc.sbuf_tensor("ytkS", [128, 4 * 2048], BF16))
        x1t0 = _sh(nc.sbuf_tensor("x1t0", [128, DT * 128], F32))
        x1t1 = _sh(nc.sbuf_tensor("x1t1", [128, DT * 128], F32))
        a1b0 = _sh(nc.sbuf_tensor("a1b0", [128, 2048], F32))
        a2b0 = _sh(nc.sbuf_tensor("a2b0", [128, 2048], F32))
        outt0 = _sh(nc.sbuf_tensor("outt0", [128, 2048], F32))
        outt1 = _sh(nc.sbuf_tensor("outt1", [128, 2048], F32))
        if True:
            rt_end = gd_y_all + [(PE, PE.n), (VE, VE.n), (AC, AC.n)]
            xn2s = [xn2s0, xn2s1]
            w1h = [w1h0, w1h1]
            w3h = [w3h0, w3h1]
            w2h = [w2h0, w2h1]
            sBs = [sB0, sB1]
            x1t = [x1t0, x1t1]
            a1b = [a1b0, a1b0]
            a2b = [a2b0, a2b0]
            outt = [outt0, outt1]
            psA = [ps0, ps1]
            psB = [ps2, ps3]
            psY = [ps4, ps5]

            ab2_hist = [0, 0]
            y2_hist = [0, 0]
            tr2_hist = [tr_hist[-2], tr_hist[-1]]
            xs_rd = [0, 0]
            ac2_si = {}
            ve2_h = {}
            ac2_ye = {}
            pe2_ybf = {}
            ld_x1h = {}
            ld_a1h = {}
            ld_a2h = {}
            ve_o1_h = {}
            pe_xt_h = {}
            ve_o_h = {}
            sc_o_h = {}
            u2 = 0
            y2 = 0
            ve2_h_last = 0
            ve_ax_last = 0
            ac_ytkS_last = 0
            for k4 in range(4 if fstop >= 3 else 0):
                ksl = k4 % 2
                ld_xs = step("sync",
                             [(XN2S[k4 // 2], sc_xn2st[k4 // 2]), (PE, xs_rd[ksl])] +
                             (rt_end if k4 == 0 else []),
                             lambda q, k4=k4, ksl=ksl: q.dma_start(
                                 out=xn2s[ksl][:].rearrange("p (a t) -> p a t", t=512),
                                 in_=xn2_scr[k4 // 2].rearrange("a p t -> p a t")[
                                     :, :, (k4 % 2) * 512:(k4 % 2 + 1) * 512]),
                             inc=XSL[k4 % 2], k=16)

                for it in range(NIT2):
                    csl = it % 2

                    def wh13ld(q, it=it, csl=csl):
                        q.dma_start(
                            out=w1h[csl][:].rearrange("p (a i) -> p a i", i=128),
                            in_=sw1T_d[:, it * 128:(it + 1) * 128].rearrange(
                                "(a p) i -> p a i", p=128)).then_inc(WH[csl].h, 16)
                        return q.dma_start(
                            out=w3h[csl][:].rearrange("p (a i) -> p a i", i=128),
                            in_=sw3T_d[:, it * 128:(it + 1) * 128].rearrange(
                                "(a p) i -> p a i", p=128))
                    WH[csl].n += 16
                    ld_wh = step("sync",
                                 [(PE, ab2_hist[-2])] +
                                 (rt_end if k4 == 0 and it < 2 else []),
                                 wh13ld, inc=WH[csl], k=16)

                    uu = u2
                    u2 += 1
                    psl = uu % 2

                    def ab2(q, it=it, csl=csl, psl=psl, ksl=ksl):
                        last = None
                        for a in range(DT):
                            q.matmul(psA[psl][:, :],
                                     w1h[csl][:, a * 128:(a + 1) * 128],
                                     xn2s[ksl][:, a * 512:(a + 1) * 512],
                                     start=(a == 0), stop=(a == DT - 1))
                        for a in range(DT):
                            last = q.matmul(psB[psl][:, :],
                                            w3h[csl][:, a * 128:(a + 1) * 128],
                                            xn2s[ksl][:, a * 512:(a + 1) * 512],
                                            start=(a == 0), stop=(a == DT - 1))
                        return last
                    pe_ab2 = step("tensor",
                                  [(WH[csl], ld_wh), (XSL[ksl], ld_xs),
                                   (AC, ac2_si.get(uu - 2,
                                                   rt_end[4][1] if uu < 2 else 0)),
                                   (VE, ve2_h.get(uu - 2, 0))],
                                  ab2, inc=PE)
                    ab2_hist.append(pe_ab2)

                    ac2_si[uu] = step("scalar",
                                      [(PE, pe_ab2), (VE, ve2_h.get(uu - 2, 0))],
                                      lambda q, psl=psl: q.activation(
                                          sBs[psl][:], psA[psl][:, :], AF.Silu),
                                      inc=AC)
                    w_h2 = [(AC, ac2_si[uu])]
                    if it == 0:
                        w_h2.append((PE, y2_hist[-1]))
                    ve2_h[uu] = step("vector", w_h2,
                                     lambda q, psl=psl, it=it: q.tensor_mul(
                                         H2[:, it * 512:(it + 1) * 512],
                                         sBs[psl][:], psB[psl][:, :]),
                                     inc=VE)
                    ve2_h_last = ve2_h[uu]
                xs_rd[ksl] = PE.n

                ac_ytkS_last = 0
                for dq in range(4):
                    def w2hld(q, dq=dq):
                        return q.dma_start(
                            out=w2h[dq % 2][:].rearrange("p (i o) -> p i o", o=512),
                            in_=sw2T_d[:, dq * 512:(dq + 1) * 512].rearrange(
                                "(i p) o -> p i o", p=128))
                    ld_w2h = step("sync",
                                  [(PE, y2_hist[-2])] +
                                  (rt_end if k4 == 0 and dq < 2 else []),
                                  w2hld, inc=W2H[dq % 2], k=16)
                    for dtl in range(4):
                        dch = dq * 4 + dtl
                        yy = y2
                        y2 += 1
                        ysl = yy % 2

                        def ymm2(q, dq=dq, dtl=dtl, ysl=ysl):
                            last = None
                            for it in range(NIT2):
                                last = q.matmul(
                                    psY[ysl][:, :],
                                    w2h[dq % 2][:, it * 512 + dtl * 128:
                                                it * 512 + (dtl + 1) * 128],
                                    H2[:, it * 512:(it + 1) * 512],
                                    start=(it == 0), stop=(it == NIT2 - 1))
                            return last
                        w_y2 = [(W2H[dq % 2], ld_w2h), (VE, ve2_h_last),
                                (AC, ac2_ye.get(yy - 2, 0))]
                        if dch == 0:
                            w_y2.append((VE, ve_ax_last))
                        pe_y2 = step("tensor", w_y2, ymm2, inc=PE)
                        if dtl == 3:
                            y2_hist.append(pe_y2)

                        ac2_ye[yy] = step("scalar",
                                          [(PE, pe_y2), (PE, pe2_ybf.get(yy - 1, 0))],
                                          lambda q, ysl=ysl: q.activation(
                                              ybf2[:], psY[ysl][:, :], AF.Copy),
                                          inc=AC)

                        tsl = len(tr2_hist) % 2

                        def ytr2(q, tsl=tsl):
                            last = None
                            for stl in range(4):
                                last = q.transpose(
                                    psTr[tsl][:, stl * 128:(stl + 1) * 128],
                                    ybf2[:, stl * 128:(stl + 1) * 128], idb[:])
                            return last
                        pe_yt2 = step("tensor", [(AC, ac2_ye[yy]), (AC, tr2_hist[-2])],
                                      ytr2, inc=PE)
                        pe2_ybf[yy] = pe_yt2

                        def ytev2(q, dch=dch, tsl=tsl):
                            return q.activation(
                                ytkS[:].rearrange("p (s w) -> p s w", w=2048)[
                                    :, :, dch * 128:(dch + 1) * 128],
                                psTr[tsl][:, 0:512].rearrange(
                                    "p (s w) -> p s w", w=128),
                                AF.Copy)
                        w_yt2 = [(PE, pe_yt2)]
                        if dch == 0:
                            w_yt2.append((VE, ve_o_h.get((k4 - 1, 3), 0)))
                        ac_yt2 = step("scalar", w_yt2, ytev2, inc=AC)
                        tr2_hist.append(ac_yt2)
                        ac_ytkS_last = ac_yt2

                for tt in range(4):
                    tsl2 = tt % 2
                    g0 = k4 * 512 + tt * 128

                    ld_a1h[(k4, tt)] = step("sync",
                                            gd_y_all +
                                            (rt_end if k4 == 0 and tt == 0 else []) +
                                            [(VE, ve_o1_h.get((k4, tt - 1),
                                              ve_o1_h.get((k4 - 1, 3), 0)))],
                                            lambda q, g0=g0, tsl2=tsl2: q.dma_start(
                                                out=a1b[tsl2][:],
                                                in_=arena_d[g0:g0 + 128, :]),
                                            inc=A1L, k=16)
                    ld_a2h[(k4, tt)] = step("sync", [],
                                            lambda q, g0=g0, tsl2=tsl2: q.dma_start(
                                                out=a2b[tsl2][:],
                                                in_=arena_d[TOK + g0:
                                                            TOK + g0 + 128, :]),
                                            inc=A2L, k=16)
                    ld_x1h[(k4, tt)] = step("sync",
                                            (rt_end if k4 == 0 and tt < 2 else []) +
                                            [(X1S[k4 % 2], sc_x1[k4]),
                                             (PE, pe_xt_h.get((k4, tt - 2),
                                              pe_xt_h.get((k4 - 1, tt + 2), 0)))],
                                            lambda q, k4=k4, tt=tt, tsl2=tsl2:
                                            q.dma_start(
                                                out=x1t[tsl2][:].rearrange(
                                                    "p (a t) -> p a t", t=128),
                                                in_=x1_scr[k4].rearrange(
                                                    "a p t -> p a t")[
                                                    :, :, tt * 128:(tt + 1) * 128]),
                                            inc=X1T[tt % 2], k=16)

                    ve_o1 = step("vector",
                                 [(A1L, ld_a1h[(k4, tt)]), (A2L, ld_a2h[(k4, tt)]),
                                  (OUTS[tt % 2], sc_o_h.get((k4, tt - 2),
                                   sc_o_h.get((k4 - 1, tt + 2), 0)))],
                                 lambda q, tsl2=tsl2: q.tensor_add(
                                     outt[tsl2][:], a1b[tsl2][:], a2b[tsl2][:]),
                                 inc=VE)
                    ve_o1_h[(k4, tt)] = ve_o1

                    ve_ox = 0
                    for xb4 in range(4):
                        xpsl = psY[xb4 % 2]

                        def xtr(q, xb4=xb4, tsl2=tsl2, xpsl=xpsl):
                            last = None
                            for i4 in range(4):
                                last = q.transpose(
                                    xpsl[:, i4 * 128:(i4 + 1) * 128],
                                    x1t[tsl2][:, (xb4 * 4 + i4) * 128:
                                              (xb4 * 4 + i4 + 1) * 128],
                                    idf[:])
                            return last
                        w_xt = [(X1T[tt % 2], ld_x1h[(k4, tt)]), (VE, ve_ox)]
                        if xb4 < 2:
                            w_xt.append((AC, ac_ytkS_last))
                        pe_xt = step("tensor", w_xt, xtr, inc=PE)
                        if xb4 == 3:
                            pe_xt_h[(k4, tt)] = pe_xt

                        ve_ox = step("vector",
                                     [(PE, pe_xt), (VE, ve_o1)],
                                     lambda q, xb4=xb4, tsl2=tsl2, xpsl=xpsl:
                                     q.tensor_add(
                                         outt[tsl2][:, xb4 * 512:(xb4 + 1) * 512],
                                         outt[tsl2][:, xb4 * 512:(xb4 + 1) * 512],
                                         xpsl[:]),
                                     inc=VE)

                    ve_o2 = step("vector",
                                 [(VE, ve_ox), (AC, ac_ytkS_last)],
                                 lambda q, tt=tt, tsl2=tsl2: q.tensor_add(
                                     outt[tsl2][:], outt[tsl2][:],
                                     ytkS[:, tt * 2048:(tt + 1) * 2048]),
                                 inc=VE)
                    ve_o_h[(k4, tt)] = ve_o2
                    ve_ax_last = ve_o2

                    sc_o_h[(k4, tt)] = step("scalar", [(VE, ve_o2)],
                                            lambda q, g0=g0, tsl2=tsl2: q.dma_start(
                                                out=yTok_d[g0:g0 + 128, :],
                                                in_=outt[tsl2][:]),
                                            inc=OUTS[tt % 2], k=16)

        sh_sb.close()
        ffn_scope.close()

    step("sync", [(sm, sm.n) for sm in DMA_SEMS], lambda q: q.nop())

    st.close()
    nc.finalize()
    return nc


_NC_CACHE = {}


def _build_cached(phases=PHASES, debug=False):
    key = (phases, debug)
    if key not in _NC_CACHE:
        _NC_CACHE[key] = build_program(phases, debug)
    return _NC_CACHE[key]


def make_host_consts():
    qp = np.arange(128)[:, None]
    c = np.arange(640)[None, :]
    band = (c >= qp + 1) & (c <= qp + 512)
    consts = {
        "idb": np.eye(128, dtype=np.float32).astype(bfdt),
        "idf": np.eye(128, dtype=np.float32),
        "id16f": np.eye(16, dtype=np.float32),
        "onerb": np.ones((1, 128), dtype=bfdt),
        "onerf": np.ones((1, 128), dtype=np.float32),
        "onecb": np.ones((128, 1), dtype=bfdt),
        "onecf": np.ones((128, 1), dtype=np.float32),
        "serow_in": np.zeros((1, 16), dtype=np.float32),
    }
    p = np.arange(128)
    consts["stu"] = (p[:, None] < p[None, :]).astype(np.float32)
    consts["ecap"] = np.broadcast_to(
        (np.arange(16) * CAP).astype(np.float32), (128, 16)).copy()
    consts["eidx"] = np.broadcast_to(
        np.arange(16).astype(np.float32), (128, 16)).copy()
    consts["si4"] = (p[:, None] + 128 * np.arange(4)[None, :]).astype(np.float32)
    consts["io1"] = p.astype(np.float32).reshape(128, 1)
    return consts, band


def make_core_inputs(x, h, band):
    own = x[h * 2048:(h + 1) * 2048]
    if h == 1:
        halo = x[2048 - 512:2048]
    else:
        halo = np.zeros((512, 2048), dtype=np.float32)
    xext = np.concatenate([halo, own], axis=0)
    xeT = np.ascontiguousarray(xext.T)
    xe = np.ascontiguousarray(xeT.reshape(DT, 128, TKV))

    masks = np.empty((5, 128, 640), dtype=np.float32)
    for qt in range(4):
        m = band.copy()
        if h == 0:
            cc = np.arange(640)[None, :]
            m = m & (qt * 128 + cc >= 512)
        masks[qt] = np.where(m, 0.0, NEG_BIG)
    masks[4] = np.where(band, 0.0, NEG_BIG)
    masks = masks.astype(bfdt)

    pos = np.arange(h * 2048 - 512, h * 2048 + 2048, dtype=np.float64)
    pos = np.maximum(pos, 0.0)
    half = 64
    theta = 1.0 / (10000.0 ** (np.arange(half, dtype=np.float64) / half))
    ang = theta[:, None] * pos[None, :]
    cos = np.concatenate([np.cos(ang), np.cos(ang)], axis=0)
    sin = np.concatenate([-np.sin(ang), np.sin(ang)], axis=0)
    return xe, masks, cos.astype(np.float32).astype(bfdt), sin.astype(np.float32).astype(bfdt)


def prep_in_maps(inputs):
    g = {k: np.asarray(v) for k, v in inputs.items()}
    x = np.ascontiguousarray(g["L0_attn_x"], dtype=np.float32)
    consts, band = make_host_consts()

    qA, kA, vA = g["L0_attn_qA"], g["L0_attn_kA"], g["L0_attn_vA"]
    A = np.zeros((96, D), dtype=np.float32)
    A[0:16], A[32:48], A[64:80] = qA, kA, vA

    shared = {
        "wqT": np.ascontiguousarray(g["L0_attn_wq"].T.astype(np.float32)),
        "wkT": np.ascontiguousarray(g["L0_attn_wk"].T.astype(np.float32)),
        "wvT": np.ascontiguousarray(g["L0_attn_wv"].T.astype(np.float32)),
        "woT": np.ascontiguousarray(g["L0_attn_wo"].T.astype(np.float32)),
        "qkvAT": np.ascontiguousarray(A.T),
        "qBT": np.ascontiguousarray(g["L0_attn_qB"].T.astype(np.float32)),
        "kBT": np.ascontiguousarray(g["L0_attn_kB"].T.astype(np.float32)),
        "vBT": np.ascontiguousarray(g["L0_attn_vB"].T.astype(np.float32)),
        "oAT": np.ascontiguousarray(g["L0_attn_oA"].T.astype(np.float32)),
        "oBT": np.ascontiguousarray(g["L0_attn_oB"].T.astype(np.float32)),
        "nwT": np.ascontiguousarray(
            g["L0_attn_norm_w"].astype(np.float32).reshape(DT, 128).T),
        "gwT": np.ascontiguousarray(g["L0_ffn_gate_w"].T.astype(np.float32)),
        "w1T": np.ascontiguousarray(g["L0_ffn_w1"].transpose(0, 2, 1)),
        "w3T": np.ascontiguousarray(g["L0_ffn_w3"].transpose(0, 2, 1)),
        "w2T": np.ascontiguousarray(g["L0_ffn_w2"].transpose(0, 2, 1)),
        "sw1T": np.ascontiguousarray(g["L0_ffn_sw1"].T),
        "sw3T": np.ascontiguousarray(g["L0_ffn_sw3"].T),
        "sw2T": np.ascontiguousarray(g["L0_ffn_sw2"].T),
    }
    shared.update(consts)

    in_maps = []
    for c in range(8):
        b, h = c // 2, c % 2
        xe, masks, cos, sin = make_core_inputs(x[b], h, band)
        m = dict(shared)
        m.update({"xe": xe, "masks": masks, "costab": cos, "sintab": sin})
        in_maps.append(m)
    return in_maps


def kernel(**inputs):
    in_maps = prep_in_maps(inputs)
    nc = _build_cached(PHASES, False)
    res = run_bass_kernel_spmd(nc, in_maps, core_ids=list(range(8)))
    out = np.empty((4, 4096, 2048), dtype=np.float32)
    for c in range(8):
        b, h = c // 2, c % 2
        yt = np.asarray(res.results[c]["yTok"], dtype=np.float32)
        out[b, h * 2048:(h + 1) * 2048, :] = yt.reshape(TOK, D)
    return out
